# version 04
# baseline (speedup 1.0000x reference)
import math
import numpy as np
import ml_dtypes
from contextlib import ExitStack

import concourse.bass as bass
import concourse.mybir as mybir
from concourse.bass import IndirectOffsetOnAxis
from concourse.bass_utils import run_bass_kernel_spmd

F32 = mybir.dt.float32
BF16 = mybir.dt.bfloat16
U32 = mybir.dt.uint32
AF = mybir.ActivationFunctionType
ALU = mybir.AluOpType
AX = mybir.AxisListType
ds = bass.ds

B, T, D = 2, 8192, 1024
NH, HD, DC, DR = 16, 64, 128, 32
NE, CAP, DI = 8, 5120, 4096
OWN = 2048
NTILE = OWN // 128
TT = B * T
SLOTS = CAP
NSC = SLOTS // 128
ISCALE = 1.0 / math.sqrt(HD + DR)
SHIFT = -25.0
OORF = 1.0e6
ENGS = ("sync", "tensor", "vector", "scalar", "gpsimd")
G8 = [list(range(8))]
G4 = [[0, 1, 2, 3], [4, 5, 6, 7]]


class Prog:
    def __init__(self, nc, es):
        self.nc, self.es = nc, es
        self.q = {e: [] for e in ENGS}
        self.sems = {}

    def sem(self, name):
        if name not in self.sems:
            self.sems[name] = [self.es.enter_context(self.nc.semaphore("s_" + name)), 0]
        return self.sems[name]

    def n(self, name):
        return self.sems[name][1] if name in self.sems else 0

    def emit(self, eng, fn, waits=(), incs=()):
        wl = [(self.sem(nm)[0], v) for nm, v in waits]
        il = []
        for nm, k in incs:
            s = self.sem(nm)
            il.append((s[0], k))
            s[1] += k

        def g(e):
            for h, v in wl:
                e.wait_ge(h, v)
            ins = fn(e)
            if il:
                if ins is None:
                    ins = e.nop()
                for h, k in il:
                    ins.then_inc(h, k)
        self.q[eng].append(g)

    def bump(self, nm, k):
        s = self.sem(nm)
        s[1] += k
        return s[0]

    def wait(self, eng, nm, v=None):
        v = self.n(nm) if v is None else v
        h = self.sem(nm)[0]
        self.q[eng].append(lambda e, h=h, v=v: e.wait_ge(h, v))


def build(debug=0, maxphase=7, p4st=20):
    nc = bass.Bass("TRN2", target_bir_lowering=False)
    es = ExitStack()

    def din(name, shape, dt=F32):
        return nc.dram_tensor(name, shape, dt, kind="ExternalInput").ap()

    def dout(name, shape, dt=F32):
        return nc.dram_tensor(name, shape, dt, kind="ExternalOutput").ap()

    def dint(name, shape, dt):
        return nc.dram_tensor(name, shape, dt).ap()

    x_own = din("x_own", [OWN, D])
    cos_own = din("cos_own", [16, OWN])
    sin_own = din("sin_own", [16, OWN])
    cos_full = din("cos_full", [16, T])
    sin_full = din("sin_full", [16, T])
    wq4_d = din("wq4", [D, 256])
    wkv_d = din("wkv", [D, DC])
    wkup4_d = din("wkup4", [DC, 256])
    wvup4_d = din("wvup4", [DC, 256])
    wqr_d = din("wqr", [D, DR])
    wkr_d = din("wkr", [DC, DR])
    wo_d = din("wo", [D, D])
    wr_d = din("wr", [NE, D])
    w1_d = din("w1e", [D, DI])
    b1_d = din("b1e", [DI])
    w2_d = din("w2e", [DI, D])
    b2_d = din("b2e", [D])
    fw1_d = din("fw1", [D, DI])
    fb1_d = din("fb1", [DI])
    fw2_d = din("fw2", [DI, D])
    fb2_d = din("fb2", [D])
    identb_d = din("identb", [128, 128], BF16)
    identf_d = din("identf", [128, 128], F32)
    uexcl_d = din("uexcl", [128, 128], BF16)
    dmask_d = din("dmask", [128, 4 * 512], BF16)
    ohsel_d = din("ohsel", [128, NE])

    out_d = dout("out_own", [OWN, D], F32)

    lg_in = dint("lg_in", [NE, OWN], F32)
    lg_ag = dint("lg_ag", [NE * 8, OWN], F32)
    ckv_in = dint("ckv_in", [160, OWN], BF16)
    ckv_ag = dint("ckv_ag", [640, OWN], BF16)
    xtok_in = dint("xtok_in", [OWN, D], BF16)
    xtok_ag = dint("xtok_ag", [TT, D], BF16)
    xd_in = dint("xd_in", [D, OWN], BF16)
    xd_ag = dint("xd_ag", [4 * D, OWN], BF16)
    ac_in = dint("ac_in", [256, T], BF16)
    ac_ag = dint("ac_ag", [1024, T], BF16)
    y_in = dint("y_in", [SLOTS, D], F32)
    y_ag = dint("y_ag", [NE * SLOTS, D], F32)
    inv_dd = dint("inv_dd", [SLOTS, 1], U32)
    yfb_d = dint("yfb_d", [OWN, D], F32)
    offa_d = dint("offa_d", [128, 128], U32)
    offb_d = dint("offb_d", [128, 128], U32)
    wab_d = dint("wab_d", [128, 3 * 128], F32)

    dbg = {}
    if debug:
        dbg["dbg_xn"] = dout("dbg_xn", [OWN, D], BF16)
        dbg["dbg_pos"] = dout("dbg_pos", [128, 128], F32)
        dbg["dbg_kept"] = dout("dbg_kept", [128, 128], F32)
        dbg["dbg_fb"] = dout("dbg_fb", [128, 128], F32)
        dbg["dbg_w8"] = dout("dbg_w8", [128, NE * 128], F32)
        dbg["dbg_offa"] = dout("dbg_offa", [128, 128], U32)
        dbg["dbg_inv"] = dout("dbg_inv", [128, NSC], U32)
        dbg["dbg_y"] = dout("dbg_y", [SLOTS, D], F32)
        dbg["dbg_yfb"] = dout("dbg_yfb", [OWN, D], F32)
        dbg["dbg_ac"] = dout("dbg_ac", [256, T], BF16)

    def S(st, name, shape, dt):
        side = "left" if (st is es or st is esW) else "right"
        return st.enter_context(nc.sbuf_tensor(name, shape, dt, side=side))

    identb = S(es, "identb_s", [128, 128], BF16)
    identf = S(es, "identf_s", [128, 128], F32)
    uexcl = S(es, "uexcl_s", [128, 128], BF16)
    ohsel = S(es, "ohsel_s", [128, NE], F32)
    onesb = S(es, "onesb", [1, 128], BF16)
    onesf = S(es, "onesf", [1, 64], F32)
    bias25 = S(es, "bias25", [128, 1], F32)
    ceps = S(es, "ceps", [128, 1], F32)
    wq4 = S(es, "wq4_s", [128, 8 * 256], BF16)
    wkv = S(es, "wkv_s", [128, 8 * DC], BF16)
    wkup4 = S(es, "wkup4_s", [128, 256], BF16)
    wvup4 = S(es, "wvup4_s", [128, 256], BF16)
    wqr = S(es, "wqr_s", [128, 8 * DR], BF16)
    wkr = S(es, "wkr_s", [128, DR], BF16)
    wrt = S(es, "wrt", [128, 64], BF16)
    b1t = S(es, "b1t", [128, 32], F32)
    b2r = S(es, "b2r", [1, D], BF16)
    fb1t = S(es, "fb1t", [128, 32], F32)
    fb2r = S(es, "fb2r", [1, D], BF16)
    xnd = S(es, "xnd", [128, 8 * OWN], BF16)
    osct = S(es, "osct", [128, 128], U32)
    invs = S(es, "invs", [128, NSC], U32)
    offa_o = S(es, "offa_o", [128, 16], U32)
    offb_o = S(es, "offb_o", [128, 16], U32)
    wab_o = S(es, "wab_o", [128, 3 * 16], F32)
    trf = S(es, "trf", [128, 128], F32)
    trfu = S(es, "trfu", [128, 128], U32)
    def PS(name, shape, dt):
        return es.enter_context(nc.psum_tensor(name, shape, dt))

    pf = [PS(f"pf{i}", [128, 512], F32) for i in range(6)]
    esP = ExitStack()
    pb0 = esP.enter_context(nc.psum_tensor("pb0", [128, 256], BF16))
    pb1 = esP.enter_context(nc.psum_tensor("pb1", [128, 256], BF16))

    esW = ExitStack()
    w1sb = S(esW, "w1sb", [128, 8 * DI], BF16)

    p = Prog(nc, es)
    ctx = {}
    chv_h = p.sem("chv")[0]
    chs_h = p.sem("chs")[0]
    chv_n = [0]
    chs_n = [0]

    class _G:
        def __init__(self, e, h, n):
            self._e, self._h, self._n = e, h, n

        def __getattr__(self, nm):
            f = getattr(self._e, nm)
            if nm in ("wait_ge", "cc_rank", "to_reg", "nop"):
                return f

            def w(*a, **k):
                ins = f(*a, **k)
                ins.then_inc(self._h, 1)
                self._n[0] += 1
                self._e.wait_ge(self._h, self._n[0])
                return ins
            return w

    def GV(e):
        return _G(e, chv_h, chv_n)

    def GS(e):
        return _G(e, chs_h, chs_n)

    esA = ExitStack()
    xs = [S(esA, f"xs{i}", [128, D], F32) for i in range(2)]
    sqscr = S(esA, "sqscr", [128, D], BF16)
    st_sum = S(esA, "st_sum", [128, 1], F32)
    st_sq = S(esA, "st_sq", [128, 1], F32)
    st_nm = S(esA, "st_nm", [128, 1], F32)
    st_var = S(esA, "st_var", [128, 1], F32)
    st_t1 = S(esA, "st_t1", [128, 1], F32)
    st_rs = S(esA, "st_rs", [128, 1], F32)
    xntok = [S(esA, f"xntok{i}", [128, D], BF16) for i in range(2)]
    ckvo = S(esA, "ckvo", [128, OWN], BF16)
    kr1 = S(esA, "kr1", [16, OWN], BF16)
    kr2 = S(esA, "kr2", [16, OWN], BF16)
    lgsb = S(esA, "lgsb", [8, OWN], F32)
    cosn = S(esA, "cosn", [16, OWN], F32)
    sinn = S(esA, "sinn", [16, OWN], F32)
    rta = S(esA, "rta", [16, 128], F32)
    rtb = S(esA, "rtb", [16, 128], F32)
    wrf = S(esA, "wrf", [8, D], F32)
    b1f = S(esA, "b1f", [32, 128], F32)
    pl_l = S(esA, "pl_l", [128, NE * 128], F32)
    pl_m = S(esA, "pl_m", [128, 128], F32)
    pl_t = S(esA, "pl_t", [128, 128], F32)
    pl_u = S(esA, "pl_u", [128, 128], F32)
    pl_x = S(esA, "pl_x", [128, 128], F32)
    pl_f1 = S(esA, "pl_f1", [128, 128], F32)
    pl_f2 = S(esA, "pl_f2", [128, 128], F32)
    pl_m1 = S(esA, "pl_m1", [128, NE * 128], F32)
    pl_m2 = S(esA, "pl_m2", [128, NE * 128], F32)
    pl_w8 = S(esA, "pl_w8", [128, NE * 128], F32)
    pl_fb = S(esA, "pl_fb", [128, 128], F32)
    pl_cnt = S(esA, "pl_cnt", [128, 128], F32)
    pl_offa = S(esA, "pl_offa", [128, 128], F32)
    pl_offb = S(esA, "pl_offb", [128, 128], F32)
    pl_wa = S(esA, "pl_wa", [128, 128], F32)
    pl_wb = S(esA, "pl_wb", [128, 128], F32)
    pl_posmy = S(esA, "pl_posmy", [128, 128], F32)
    pl_keptmy = S(esA, "pl_keptmy", [128, 128], F32)
    pl_elig = S(esA, "pl_elig", [128, 128], F32)
    pl_scan = S(esA, "pl_scan", [128, 128], F32)
    pl_bs = S(esA, "pl_bs", [128, 1], BF16)
    pl_excl = S(esA, "pl_excl", [128, 1], F32)
    pl_pos = S(esA, "pl_pos", [128, 128], F32)
    pl_kept = S(esA, "pl_kept", [128, 128], F32)
    pl_sel = S(esA, "pl_sel", [128, 128], F32)
    pl_osc = S(esA, "pl_osc", [128, 128], F32)
    payj = S(esA, "payj", [128, 128], U32)
    pl_iu = S(esA, "pl_iu", [128, 128], U32)
    pl_iv = S(esA, "pl_iv", [128, 128], U32)
    zrow = S(esA, "zrow", [128, NSC], U32)

    for dst, src in [(identb, identb_d), (identf, identf_d), (uexcl, uexcl_d),
                     (ohsel, ohsel_d), (cosn, cos_own), (sinn, sin_own),
                     (wrf, wr_d)]:
        p.emit("sync", lambda e, d=dst, s=src: e.dma_start(out=d[:], in_=s[:]),
               incs=[("consts", 16)])
    N_CD = p.n("consts")

    for tile, val in [(onesb, 1.0), (onesf, 1.0), (bias25, SHIFT), (ceps, 1e-5)]:
        p.emit("vector", lambda e, t=tile, v=val: (GV(e).memset(t[:], v), None)[1],
               incs=[("msets", 1)])
    for tile, val in [(pl_fb, 0.0), (pl_cnt, 0.0), (pl_offa, OORF), (pl_offb, OORF),
                      (pl_wa, 0.0), (pl_wb, 0.0), (pl_posmy, 0.0), (pl_keptmy, 0.0),
                      (pl_f1, 0.0), (pl_f2, 0.0)]:
        p.emit("gpsimd", lambda e, t=tile, v=val: e.memset(t[:], v), incs=[("msets2", 1)])
    p.emit("gpsimd", lambda e: e.memset(zrow[:], 0), incs=[("msets2", 1)])
    N_MS2 = p.n("msets2")

    def wload(dst_ap, src_ap, waits=()):
        p.emit("gpsimd", lambda e, d=dst_ap, s=src_ap: e.dma_start(out=d, in_=s),
               waits=waits, incs=[("wload", 16)])

    def wload_ch(dst, src, rows, cols, waits=()):
        a = rows // 128
        wload(dst[:, 0:a * cols].rearrange("q (a c) -> q a c", a=a),
              src.rearrange("(a q) c -> q a c", q=128), waits=waits)

    wload_ch(wq4, wq4_d, D, 256)
    wload_ch(wkv, wkv_d, D, DC)
    wload(wkup4[:], wkup4_d[:])
    wload(wvup4[:], wvup4_d[:])
    wload_ch(wqr, wqr_d, D, DR)
    wload(wkr[:], wkr_d[:])
    wload(b2r[:], b2_d[None, :])
    wload(fb2r[:], fb2_d[None, :])
    N_WSMALL = p.n("wload")
    wload_ch(w1sb, w1_d, D, DI)
    N_W1 = p.n("wload")
    p.emit("gpsimd", lambda e: e.dma_start(
        out=inv_dd.rearrange("(b a) c -> b (a c)", b=128), in_=zrow[:]),
        waits=[("msets2", N_MS2)], incs=[("inv0", 16)])

    for ci in range(8):
        p.emit("tensor", lambda e, ci=ci: e.transpose(
            pf[5][0:128, 0:8], wrf[0:8, 128 * ci:128 * (ci + 1)], identf[0:8, 0:8]),
            waits=[("consts", N_CD)] + ([("wrt_v", ci)] if ci else []),
            incs=[("wrt_pe", 1)])
        p.emit("vector", lambda e, ci=ci: (GV(e).tensor_copy(
            wrt[:, 8 * ci:8 * (ci + 1)], pf[5][0:128, 0:8]), None)[1],
            waits=[("wrt_pe", ci + 1)], incs=[("wrt_v", 1)])
    for bsrc, bdst in [(b1_d, b1t), (fb1_d, fb1t)]:
        p.emit("sync", lambda e, s=bsrc: e.dma_start(
            out=b1f[:], in_=s.rearrange("(a c) -> a c", c=128)),
            waits=[("bT_pe", p.n("bT_pe"))], incs=[("bT_ld", 16)])
        p.emit("tensor", lambda e: e.transpose(pf[5][0:128, 0:32], b1f[:], identf[0:32, 0:32]),
               waits=[("bT_ld", p.n("bT_ld")), ("wrt_v", 8)], incs=[("bT_pe", 1)])
        p.emit("vector", lambda e, d=bdst: (GV(e).tensor_copy(d[:], pf[5][0:128, 0:32]), None)[1],
               waits=[("bT_pe", p.n("bT_pe"))], incs=[("bT_v", 1)])
    N_BT = p.n("bT_v")

    for i in range(NTILE):
        bi = i % 2
        w = [("ln_v", 3 * (i - 1))] if i >= 2 else []
        p.emit("sync", lambda e, i=i, bi=bi: e.dma_start(
            out=xs[bi][:], in_=x_own[128 * i:128 * (i + 1), :]), waits=w,
            incs=[("x_ld", 16)])
        p.emit("scalar", lambda e, bi=bi: (GS(e).activation(
            sqscr[:], xs[bi][:], AF.Square, accum_out=st_sq[:]), None)[1],
            waits=[("x_ld", 16 * (i + 1))] + ([("ln_v", 3 * i)] if i else []),
            incs=[("ln_s", 1)])
        p.emit("vector", lambda e, bi=bi: (GV(e).reduce_sum(st_sum[:], xs[bi][:], axis=AX.X), None)[1],
               waits=[("x_ld", 16 * (i + 1))] + ([("ln_s", 3 * i)] if i else []),
               incs=[("ln_v", 1)])

        def stats(e):
            e = GV(e)
            e.tensor_scalar_mul(st_nm[:], st_sum[:], -1.0 / D)
            e.tensor_tensor(st_t1[:], st_nm[:], st_nm[:], ALU.mult)
            e.tensor_scalar_mul(st_var[:], st_sq[:], 1.0 / D)
            e.tensor_tensor(st_var[:], st_var[:], st_t1[:], ALU.subtract)
        p.emit("vector", stats, waits=[("ln_s", 3 * i + 1)], incs=[("ln_v", 1)])

        def rstd(e):
            e = GS(e)
            e.activation(st_t1[:], st_var[:], AF.Ln, bias=ceps[:])
            e.activation(st_rs[:], st_t1[:], AF.Exp, scale=-0.5)
        p.emit("scalar", rstd, waits=[("ln_v", 3 * i + 2), ("msets", 4)],
               incs=[("ln_s", 2)])
        wxn = [("ln_s", 3 * i + 3)]
        if i >= 2:
            wxn += [("xn_dma", 16 * (i - 1)), ("tr_pe", 8 * (i - 1))]
        p.emit("vector", lambda e, bi=bi: e.tensor_scalar(
            xntok[bi][:], xs[bi][:], st_nm[:], st_rs[:], ALU.add, ALU.mult),
            waits=wxn, incs=[("ln_v", 1)])
        lnv = 3 * (i + 1)
        p.emit("sync", lambda e, i=i, bi=bi: e.dma_start(
            out=xtok_in[128 * i:128 * (i + 1), :], in_=xntok[bi][:]),
            waits=[("ln_v", lnv)], incs=[("xn_dma", 16)])
        if debug:
            p.emit("sync", lambda e, i=i, bi=bi: e.dma_start(
                out=dbg["dbg_xn"][128 * i:128 * (i + 1), :], in_=xntok[bi][:]),
                waits=[("ln_v", lnv)], incs=[("dbg_s", 16)])
        for ci in range(8):
            tcnt = i * 8 + ci
            pbx = pb0 if tcnt % 2 == 0 else pb1
            w = [("ln_v", lnv)]
            if tcnt >= 2:
                w.append(("tr_v", tcnt - 1))
            p.emit("tensor", lambda e, bi=bi, ci=ci, pbx=pbx: e.transpose(
                pbx[:, 0:128], xntok[bi][:, 128 * ci:128 * (ci + 1)], identb[:]),
                waits=w, incs=[("tr_pe", 1)])
            p.emit("vector", lambda e, i=i, ci=ci, pbx=pbx: (GV(e).tensor_copy(
                xnd[:, 2048 * ci + 128 * i:2048 * ci + 128 * (i + 1)], pbx[:, 0:128]), None)[1],
                waits=[("tr_pe", tcnt + 1)], incs=[("tr_v", 1)])
        trv = 8 * (i + 1)
        for ci in range(8):
            w = []
            if ci == 0:
                w = [("tr_v", trv), ("wrt_v", 8)]
                if i >= 1:
                    w.append(("proj_v", 2 * i))
            p.emit("tensor", lambda e, ci=ci, i=i: e.matmul(
                pf[4][0:8, 0:128], wrt[:, 8 * ci:8 * (ci + 1)],
                xnd[:, 2048 * ci + 128 * i:2048 * ci + 128 * (i + 1)],
                start=(ci == 0), stop=(ci == 7)),
                waits=w, incs=([("proj_pe", 1)] if ci == 7 else []))
        for ci in range(8):
            w = [("wload", 32)] if (ci == 0 and i == 0) else []
            p.emit("tensor", lambda e, ci=ci, i=i: e.matmul(
                pf[3][0:128, 0:128], wkv[:, 128 * ci:128 * (ci + 1)],
                xnd[:, 2048 * ci + 128 * i:2048 * ci + 128 * (i + 1)],
                start=(ci == 0), stop=(ci == 7)),
                waits=w, incs=([("proj_pe", 1)] if ci == 7 else []))
        p.emit("vector", lambda e, i=i: (GV(e).tensor_copy(
            lgsb[0:8, 128 * i:128 * (i + 1)], pf[4][0:8, 0:128]), None)[1],
            waits=[("proj_pe", 2 * i + 1)], incs=[("proj_v", 1)])
        p.emit("vector", lambda e, i=i: (GV(e).tensor_copy(
            ckvo[:, 128 * i:128 * (i + 1)], pf[3][0:128, 0:128]), None)[1],
            waits=[("proj_pe", 2 * i + 2)], incs=[("proj_v", 1)])
        for half in range(2):
            w = [("proj_v", 2 * (i + 1))]
            if i == 0 and half == 0:
                w.append(("wload", 96))
            if i >= 1 and half == 0:
                w.append(("kr_v", 6 * i))
            p.emit("tensor", lambda e, i=i, half=half: e.matmul(
                pf[half][0:16, 0:128], wkr[:, 16 * half:16 * (half + 1)],
                ckvo[:, 128 * i:128 * (i + 1)], start=True, stop=True),
                waits=w, incs=[("kr_pe", 1)])

        def kro(e, i=i):
            e = GV(e)
            c0, c1 = 128 * i, 128 * (i + 1)
            e.tensor_tensor(rta[:], pf[0][0:16, 0:128], cosn[:, c0:c1], ALU.mult)
            e.tensor_tensor(rtb[:], pf[1][0:16, 0:128], sinn[:, c0:c1], ALU.mult)
            e.tensor_tensor(kr1[:, c0:c1], rta[:], rtb[:], ALU.subtract)
            e.tensor_tensor(rta[:], pf[1][0:16, 0:128], cosn[:, c0:c1], ALU.mult)
            e.tensor_tensor(rtb[:], pf[0][0:16, 0:128], sinn[:, c0:c1], ALU.mult)
            e.tensor_tensor(kr2[:, c0:c1], rta[:], rtb[:], ALU.add)
        p.emit("vector", kro, waits=[("kr_pe", 2 * (i + 1)), ("consts", N_CD)],
               incs=[("kr_v", 6)])

    p.emit("sync", lambda e: e.dma_start(out=ckv_in[0:128, :], in_=ckvo[:]),
           waits=[("proj_v", 2 * NTILE)], incs=[("p1d", 16)])
    p.emit("sync", lambda e: e.dma_start(out=ckv_in[128:144, :], in_=kr1[:]),
           waits=[("kr_v", 6 * NTILE)], incs=[("p1d", 16)])
    p.emit("sync", lambda e: e.dma_start(out=ckv_in[144:160, :], in_=kr2[:]),
           incs=[("p1d", 16)])
    p.emit("sync", lambda e: e.dma_start(out=lg_in[:], in_=lgsb[:]),
           waits=[("proj_v", 2 * NTILE)], incs=[("p1d", 16)])
    p.emit("sync", lambda e: e.dma_start(
        out=xd_in.rearrange("(a q) t -> q a t", q=128),
        in_=xnd.rearrange("q (a t) -> q a t", a=8)),
        waits=[("tr_v", 8 * NTILE)], incs=[("p1d", 16)])

    p.emit("gpsimd", lambda e: e.collective_compute(
        "AllGather", ALU.bypass, replica_groups=G8, ins=[lg_in.opt()], outs=[lg_ag.opt()]),
        waits=[("p1d", 64), ("wload", N_W1), ("inv0", 16)], incs=[("ag_lg", 1)])
    p.emit("gpsimd", lambda e: e.collective_compute(
        "AllGather", ALU.bypass, replica_groups=G8, ins=[xtok_in.opt()], outs=[xtok_ag.opt()]),
        waits=[("xn_dma", 16 * NTILE)], incs=[("ag_xt", 1)])
    p.emit("gpsimd", lambda e: e.collective_compute(
        "AllGather", ALU.bypass, replica_groups=G4, ins=[xd_in.opt()], outs=[xd_ag.opt()]),
        waits=[("p1d", 80)], incs=[("ag_xd", 1)])
    p.emit("gpsimd", lambda e: e.collective_compute(
        "AllGather", ALU.bypass, replica_groups=G4, ins=[ckv_in.opt()], outs=[ckv_ag.opt()]),
        incs=[("ag_ck", 1)])

    if maxphase < 2:
        es2_ = [esA, esW]
        for st_ in es2_:
            st_.close()
        _finalize(nc, es, p, block_gate=True)
        return nc
    for rp in range(8):
        p.emit("sync", lambda e, rp=rp: e.dma_start(
            out=pl_l[16 * rp:16 * (rp + 1), :].rearrange("j (a i) -> j a i", a=NE),
            in_=lg_ag[NE * rp:NE * (rp + 1), :].rearrange("a (j i) -> j a i", i=128)),
            waits=[("ag_lg", 1)], incs=[("pl_ld", 16)])

    if maxphase < 2.1:
        esA.close(); esW.close()
        _finalize(nc, es, p, block_gate=True)
        return nc

    def pe_(i):
        return pl_l[:, 128 * i:128 * (i + 1)]

    def rv(fn, waits=()):
        p.emit("vector", lambda e, fn=fn: (fn(GV(e)), None)[1], waits=waits,
               incs=[("rt_v", 1)])

    rv(lambda e: e.tensor_copy(pl_m[:], pe_(0)),
       waits=[("pl_ld", 128), ("msets2", N_MS2)])
    for i in range(1, 8):
        rv(lambda e, i=i: e.tensor_tensor(pl_m[:], pl_m[:], pe_(i), ALU.max))
    for i in range(8):
        def m1f(e, i=i):
            e.tensor_tensor(pl_m1[:, 128 * i:128 * (i + 1)], pe_(i), pl_m[:], ALU.is_ge)
            e.tensor_scalar(pl_t[:], pl_f1[:], -1.0, 1.0, ALU.mult, ALU.add)
            e.tensor_tensor(pl_m1[:, 128 * i:128 * (i + 1)],
                            pl_m1[:, 128 * i:128 * (i + 1)], pl_t[:], ALU.mult)
            return e.tensor_tensor(pl_f1[:], pl_f1[:],
                                   pl_m1[:, 128 * i:128 * (i + 1)], ALU.add)
        rv(m1f)
    for i in range(8):
        def mska(e, i=i):
            e.tensor_scalar_mul(pl_t[:], pl_m1[:, 128 * i:128 * (i + 1)], 1.0e30)
            return e.tensor_tensor(pl_m2[:, 128 * i:128 * (i + 1)], pe_(i), pl_t[:],
                                   ALU.subtract)
        rv(mska)
    rv(lambda e: e.tensor_copy(pl_u[:], pl_m2[:, 0:128]))
    for i in range(1, 8):
        rv(lambda e, i=i: e.tensor_tensor(pl_u[:], pl_u[:],
                                          pl_m2[:, 128 * i:128 * (i + 1)], ALU.max))
    for i in range(8):
        def m2f(e, i=i):
            e.tensor_tensor(pl_m2[:, 128 * i:128 * (i + 1)],
                            pl_m2[:, 128 * i:128 * (i + 1)], pl_u[:], ALU.is_ge)
            e.tensor_scalar(pl_t[:], pl_f2[:], -1.0, 1.0, ALU.mult, ALU.add)
            e.tensor_tensor(pl_m2[:, 128 * i:128 * (i + 1)],
                            pl_m2[:, 128 * i:128 * (i + 1)], pl_t[:], ALU.mult)
            return e.tensor_tensor(pl_f2[:], pl_f2[:],
                                   pl_m2[:, 128 * i:128 * (i + 1)], ALU.add)
        rv(m2f)
    rv(lambda e: e.tensor_tensor(pl_t[:], pl_u[:], pl_m[:], ALU.subtract))
    p.emit("scalar", lambda e: (GS(e).activation(pl_x[:], pl_t[:], AF.Exp), None)[1],
           waits=[("rt_v", p.n("rt_v"))], incs=[("rt_s", 1)])

    def wts(e):
        e.tensor_scalar_add(pl_t[:], pl_x[:], 1.0)
        e.reciprocal(pl_t[:], pl_t[:])
        e.tensor_tensor(pl_u[:], pl_x[:], pl_t[:], ALU.mult)
        return e.tensor_scalar(pl_x[:], pl_u[:], -1.0, 1.0, ALU.mult, ALU.add)
    rv(wts, waits=[("rt_s", 1)])
    for i in range(8):
        def w8f(e, i=i):
            e.tensor_tensor(pl_w8[:, 128 * i:128 * (i + 1)],
                            pl_m1[:, 128 * i:128 * (i + 1)], pl_x[:], ALU.mult)
            e.tensor_tensor(pl_t[:], pl_m2[:, 128 * i:128 * (i + 1)], pl_u[:], ALU.mult)
            return e.tensor_tensor(pl_w8[:, 128 * i:128 * (i + 1)],
                                   pl_w8[:, 128 * i:128 * (i + 1)], pl_t[:], ALU.add)
        rv(w8f)

    if maxphase < 2.3:
        esA.close(); esW.close()
        _finalize(nc, es, p, block_gate=True)
        return nc
    for ee in range(NE):
        def rnd_a(e, ee=ee):
            e.tensor_tensor(pl_elig[:], pl_m1[:, 128 * ee:128 * (ee + 1)],
                            pl_m2[:, 128 * ee:128 * (ee + 1)], ALU.add)
            e.tensor_scalar(pl_t[:], pl_fb[:], -1.0, 1.0, ALU.mult, ALU.add)
            e.tensor_tensor(pl_elig[:], pl_elig[:], pl_t[:], ALU.mult)
            e.tensor_tensor_scan(pl_scan[:], pl_elig[:], pl_elig[:], 0.0,
                                 ALU.add, ALU.bypass)
            return e.tensor_copy(pl_bs[:], pl_scan[:, 127:128])
        rv(rnd_a)
        p.emit("tensor", lambda e: e.matmul(pf[5][0:128, 0:1], uexcl[:], pl_bs[:],
                                            start=True, stop=True),
               waits=[("rt_v", p.n("rt_v"))] + ([("rt_v2", p.n("rt_v2"))] if ee else []),
               incs=[("rt_pe", 1)])

        def rnd_b(e, ee=ee):
            e = GV(e)
            e.tensor_copy(pl_excl[:], pf[5][0:128, 0:1])
            e.tensor_scalar(pl_pos[:], pl_scan[:], pl_excl[:], None, ALU.add)
            e.tensor_scalar(pl_t[:], pl_pos[:], float(CAP), None, ALU.is_le)
            e.tensor_tensor(pl_kept[:], pl_t[:], pl_elig[:], ALU.mult)
            e.tensor_tensor(pl_t[:], pl_elig[:], pl_kept[:], ALU.subtract)
            e.tensor_tensor(pl_fb[:], pl_fb[:], pl_t[:], ALU.add)
            e.tensor_scalar(pl_u[:], pl_pos[:], -1.0 + ee * SLOTS, None, ALU.add)
            e.tensor_scalar(pl_t[:], pl_cnt[:], 0.0, None, ALU.is_equal)
            e.tensor_tensor(pl_sel[:], pl_t[:], pl_kept[:], ALU.mult)
            e.tensor_tensor(pl_x[:], pl_u[:], pl_offa[:], ALU.subtract)
            e.tensor_tensor(pl_x[:], pl_x[:], pl_sel[:], ALU.mult)
            e.tensor_tensor(pl_offa[:], pl_offa[:], pl_x[:], ALU.add)
            e.tensor_tensor(pl_x[:], pl_w8[:, 128 * ee:128 * (ee + 1)], pl_wa[:],
                            ALU.subtract)
            e.tensor_tensor(pl_x[:], pl_x[:], pl_sel[:], ALU.mult)
            e.tensor_tensor(pl_wa[:], pl_wa[:], pl_x[:], ALU.add)
            e.tensor_scalar(pl_t[:], pl_cnt[:], 1.0, None, ALU.is_equal)
            e.tensor_tensor(pl_sel[:], pl_t[:], pl_kept[:], ALU.mult)
            e.tensor_tensor(pl_x[:], pl_u[:], pl_offb[:], ALU.subtract)
            e.tensor_tensor(pl_x[:], pl_x[:], pl_sel[:], ALU.mult)
            e.tensor_tensor(pl_offb[:], pl_offb[:], pl_x[:], ALU.add)
            e.tensor_tensor(pl_x[:], pl_w8[:, 128 * ee:128 * (ee + 1)], pl_wb[:],
                            ALU.subtract)
            e.tensor_tensor(pl_x[:], pl_x[:], pl_sel[:], ALU.mult)
            e.tensor_tensor(pl_wb[:], pl_wb[:], pl_x[:], ALU.add)
            e.tensor_tensor(pl_cnt[:], pl_cnt[:], pl_kept[:], ALU.add)
            e.tensor_scalar(pl_t[:], pl_pos[:], ohsel[:, ee:ee + 1], None, ALU.mult)
            e.tensor_tensor(pl_posmy[:], pl_posmy[:], pl_t[:], ALU.add)
            e.tensor_scalar(pl_t[:], pl_kept[:], ohsel[:, ee:ee + 1], None, ALU.mult)
            e.tensor_tensor(pl_keptmy[:], pl_keptmy[:], pl_t[:], ALU.add)
        p.emit("vector", rnd_b, waits=[("rt_pe", ee + 1)], incs=[("rt_v2", 1)])

    def post(e):
        e = GV(e)
        e.tensor_scalar(pl_t[:], pl_fb[:], -1.0, 1.0, ALU.mult, ALU.add)
        e.tensor_tensor(pl_wa[:], pl_wa[:], pl_t[:], ALU.mult)
        e.tensor_tensor(pl_wb[:], pl_wb[:], pl_t[:], ALU.mult)
        e.tensor_scalar(pl_u[:], pl_posmy[:], -1.0, 0.0, ALU.add, ALU.max)
        e.tensor_copy(pl_iu[:], pl_u[:])
        e.tensor_scalar(pl_iv[:], pl_iu[:], 127, None, ALU.bitwise_and)
        e.tensor_copy(pl_x[:], pl_iv[:])
        e.tensor_scalar(pl_iv[:], pl_iu[:], 7, None, ALU.logical_shift_right)
        e.tensor_copy(pl_t[:], pl_iv[:])
        e.tensor_scalar(pl_osc[:], pl_x[:], float(NSC), None, ALU.mult)
        e.tensor_tensor(pl_osc[:], pl_osc[:], pl_t[:], ALU.add)
        e.tensor_scalar_add(pl_osc[:], pl_osc[:], -OORF)
        e.tensor_tensor(pl_osc[:], pl_osc[:], pl_keptmy[:], ALU.mult)
        e.tensor_scalar_add(pl_osc[:], pl_osc[:], OORF)
    p.emit("vector", post, waits=[("rt_v2", 8)], incs=[("rt_v3", 1)])

    if debug:
        for nm, src in [("dbg_pos", pl_posmy), ("dbg_kept", pl_keptmy),
                        ("dbg_fb", pl_fb), ("dbg_w8", pl_w8)]:
            p.emit("sync", lambda e, nm=nm, s=src: e.dma_start(out=dbg[nm][:], in_=s[:]),
                   waits=[("rt_v3", 1)], incs=[("dbg_s", 16)])

    if maxphase < 2.6:
        esA.close(); esW.close()
        _finalize(nc, es, p, block_gate=True)
        return nc
    plans = [(pl_osc, osct, None), (pl_offa, trfu, offa_d[:]), (pl_offb, trfu, offb_d[:]),
             (pl_wa, trf, wab_d[:, 0:128]), (pl_wb, trf, wab_d[:, 128:256]),
             (pl_fb, trf, wab_d[:, 256:384])]
    for k, (spl, ssb, sdr) in enumerate(plans):
        w = [("rt_v3", 1)]
        if k:
            w.append(("tp_v", k))
        p.emit("tensor", lambda e, s=spl: e.transpose(pf[5][0:128, 0:128], s[:], identf[:]),
               waits=w, incs=[("tp_pe", 1)])
        w2_ = [("tp_pe", k + 1)]
        if k >= 2:
            w2_.append(("tp_dma", p.n("tp_dma")))
        p.emit("vector", lambda e, d=ssb: (GV(e).tensor_copy(d[:], pf[5][0:128, 0:128]), None)[1],
               waits=w2_, incs=[("tp_v", 1)])
        if sdr is not None:
            p.emit("sync", lambda e, sb=ssb, dr=sdr: e.dma_start(out=dr, in_=sb[:]),
                   waits=[("tp_v", p.n("tp_v"))], incs=[("tp_dma", 16)])
    N_TPD = p.n("tp_dma")

    if maxphase < 2.8:
        esA.close(); esW.close()
        _finalize(nc, es, p, block_gate=True)
        return nc

    def mkrank(e):
        ctx["rk8"] = e.cc_rank(G8)
        ctx["rk4"] = e.cc_rank(G4)
        return e.dma_start(out=offa_o[:], in_=offa_d[:, ds(ctx["rk8"] * 16, 16)])
    p.emit("sync", mkrank, waits=[("tp_dma", N_TPD)], incs=[("own_ld", 16)])
    p.emit("sync", lambda e: e.dma_start(out=offb_o[:],
                                         in_=offb_d[:, ds(ctx["rk8"] * 16, 16)]),
           incs=[("own_ld", 16)])
    for k in range(3):
        p.emit("sync", lambda e, k=k: e.dma_start(
            out=wab_o[:, 16 * k:16 * (k + 1)],
            in_=wab_d[:, ds(ctx["rk8"] * 16 + 128 * k, 16)]),
            incs=[("own_ld", 16)])
    N_OWNLD = p.n("own_ld")

    if debug:
        p.emit("sync", lambda e: e.dma_start(out=dbg["dbg_offa"][:], in_=offa_d[:]),
               waits=[("tp_dma", 32)], incs=[("dbg_s", 16)])

    if maxphase < 3:
        esA.close(); esW.close()
        _finalize(nc, es, p, block_gate=True)
        return nc
    p.emit("gpsimd", lambda e: e.iota(payj[:], [[128, 128]], base=0, channel_multiplier=1),
           incs=[("disp", 1)])

    sc_h = p.bump("scat", 16 * 128)

    def scat_all(e):
        bc = e.to_reg(SLOTS - 1)
        for j in range(128):
            e.indirect_dma_start(
                out=inv_dd[:], out_offset=IndirectOffsetOnAxis(ap=osct[:, j:j + 1], axis=0),
                in_=payj[:, j:j + 1], in_offset=None,
                bounds_check=bc, oob_is_err=False).then_inc(sc_h, 16)
        return e.nop()
    p.emit("gpsimd", scat_all, waits=[("tp_v", 1), ("disp", 1), ("inv0", 16), ("ag_lg", 1), ("ag_xt", 1), ("ag_xd", 1), ("ag_ck", 1)])
    p.emit("gpsimd", lambda e: e.dma_start(
        out=invs[:], in_=inv_dd.rearrange("(b a) c -> b (a c)", b=128)),
        waits=[("scat", 16 * 128)], incs=[("invl", 16)])
    if debug:
        p.emit("sync", lambda e: e.dma_start(out=dbg["dbg_inv"][:], in_=invs[:]),
               waits=[("invl", 16)], incs=[("dbg_s", 16)])

    if maxphase < 4:
        esA.close(); esW.close()
        _finalize(nc, es, p, block_gate=True)
        return nc
    FENB = [("rt_v3", 1), ("tp_v", 6), ("invl", 16), ("kr_v", 6 * NTILE),
            ("proj_v", 2 * NTILE), ("ln_v", 3 * NTILE), ("bT_v", N_BT),
            ("wrt_v", 8), ("dbg_s", p.n("dbg_s"))]
    esA.close()
    esB = ExitStack()
    w2sb = S(esB, "w2sb", [128, 32 * D], BF16)
    hsb = S(esB, "hsb", [128, 32 * 256], BF16)
    ysb = S(esB, "ysb", [128, D], F32)
    xg = [S(esB, f"xg{i}", [128, D], BF16) for i in range(2)]
    xdt = S(esB, "xdt", [128, 8 * 256], BF16)

    wload(w2sb.rearrange("q (a c) -> q a c", a=32),
          w2_d.rearrange("(a q) c -> q a c", q=128),
          waits=FENB + [("ag_lg", 1), ("ag_xt", 1), ("ag_xd", 1), ("ag_ck", 1)])
    N_W2 = p.n("wload")

    p.emit("gpsimd", lambda e: (ctx.__setitem__("bcg", e.to_reg(TT - 1)), e.nop())[1],
           waits=FENB, incs=[("bcr", 1)])
    for gi in range(2):
        p.emit("vector", lambda e, gi=gi: (GV(e).memset(xg[gi][:], 0.0), None)[1],
               waits=(FENB if gi == 0 else []), incs=[("xgz", 1)])

    for st in range(p4st):
        for kk in range(2):
            k = 2 * st + kk
            w = [("ag_xt", 1), ("invl", 16), ("bcr", 1), ("xgz", 2)]
            if k >= 2:
                w.append(("xdv", 8 * (k - 1)))
            p.emit("gpsimd", lambda e, k=k: e.indirect_dma_start(
                out=xg[k % 2][:], out_offset=None, in_=xtok_ag[:],
                in_offset=IndirectOffsetOnAxis(ap=invs[:, k:k + 1], axis=0),
                bounds_check=ctx["bcg"], oob_is_err=False),
                waits=w, incs=[("gx", 16)])
            for ci in range(8):
                tcnt = k * 8 + ci
                pbx = pb0 if tcnt % 2 == 0 else pb1
                w = [("gx", 16 * (k + 1))]
                if tcnt >= 2:
                    w.append(("xdv", tcnt - 1))
                p.emit("tensor", lambda e, k=k, ci=ci, pbx=pbx: e.transpose(
                    pbx[:, 0:128], xg[k % 2][:, 128 * ci:128 * (ci + 1)], identb[:]),
                    waits=w, incs=[("xdtr", 1)])
                wv = [("xdtr", tcnt + 1)]
                if st >= 1 and kk == 0 and ci == 0:
                    wv.append(("hml", 32 * st))
                p.emit("vector", lambda e, kk=kk, ci=ci, pbx=pbx: (GV(e).tensor_copy(
                    xdt[:, 256 * ci + 128 * kk:256 * ci + 128 * (kk + 1)], pbx[:, 0:128]), None)[1],
                    waits=wv, incs=[("xdv", 1)])
        for ic in range(32):
            pfh = pf[ic % 2]
            hcnt = st * 32 + ic
            for ci in range(8):
                w = []
                if ci == 0:
                    w = [("xdv", 16 * (st + 1))]
                    if st == 0 and ic == 0:
                        w.append(("wload", N_W1))
                    if hcnt >= 2:
                        w.append(("gel", hcnt - 1))
                p.emit("tensor", lambda e, ci=ci, ic=ic, pfh=pfh: e.matmul(
                    pfh[0:128, 0:256],
                    w1sb[:, 4096 * ci + 128 * ic:4096 * ci + 128 * (ic + 1)],
                    xdt[:, 256 * ci:256 * (ci + 1)], start=(ci == 0), stop=(ci == 7)),
                    waits=w, incs=([("hml", 1)] if ci == 7 else []))
            wsc = [("hml", hcnt + 1)]
            if st == 0 and ic == 0:
                wsc.append(("bT_v", N_BT))
            if st >= 1 and ic == 0:
                wsc.append(("ymm", 4 * st))
            p.emit("scalar", lambda e, ic=ic, pfh=pfh: (GS(e).activation(
                hsb[:, 256 * ic:256 * (ic + 1)], pfh[0:128, 0:256], AF.Gelu,
                bias=b1t[:, ic:ic + 1]), None)[1], waits=wsc, incs=[("gel", 1)])
        for ss in range(2):
            for dh in range(2):
                pfy = pf[2 + (2 * ss + dh) % 2]
                ycnt = st * 4 + ss * 2 + dh
                for ic in range(32):
                    w = []
                    if ic == 0:
                        w = [("gel", 32 * (st + 1))]
                        if st == 0 and ss == 0 and dh == 0:
                            w.append(("wload", N_W2))
                        if ycnt >= 2:
                            w.append(("yev", ycnt - 1))
                    p.emit("tensor", lambda e, ss=ss, dh=dh, ic=ic, pfy=pfy: e.matmul(
                        pfy[:], hsb[:, 256 * ic + 128 * ss:256 * ic + 128 * (ss + 1)],
                        w2sb[:, 1024 * ic + 512 * dh:1024 * ic + 512 * (dh + 1)],
                        start=(ic == 0), stop=False), waits=w)
                w = [("msets", 4), ("wload", 112)] if ycnt == 0 else []
                p.emit("tensor", lambda e, dh=dh, pfy=pfy: e.matmul(
                    pfy[:], onesb[0:1, 0:128], b2r[0:1, 512 * dh:512 * (dh + 1)],
                    start=False, stop=True), waits=w, incs=[("ymm", 1)])
                p.emit("vector", lambda e, dh=dh, pfy=pfy: (GV(e).tensor_copy(
                    ysb[:, 512 * dh:512 * (dh + 1)], pfy[:]), None)[1],
                    waits=[("ymm", ycnt + 1), ("ydma", 16 * (st * 2 + ss))],
                    incs=[("yev", 1)])
            p.emit("sync", lambda e, st=st, ss=ss: e.dma_start(
                out=y_in[256 * st + 128 * ss:256 * st + 128 * (ss + 1), :], in_=ysb[:]),
                waits=[("yev", st * 4 + 2 * (ss + 1))], incs=[("ydma", 16)])
    N_YDMA = p.n("ydma")

    p.emit("gpsimd", lambda e: e.collective_compute(
        "AllGather", ALU.bypass, replica_groups=G8, ins=[y_in.opt()], outs=[y_ag.opt()]),
        waits=[("ydma", N_YDMA)], incs=[("ag_y", 1)])
    if debug:
        p.emit("sync", lambda e: e.dma_start(out=dbg["dbg_y"][:], in_=y_in[:]),
               waits=[("ydma", N_YDMA)], incs=[("dbg_s", 16)])

    if maxphase < 5:
        esB.close(); esW.close()
        _finalize(nc, es, p, block_gate=True)
        return nc
    FENC = [("ymm", p.n("ymm")), ("gel", p.n("gel")), ("xdv", p.n("xdv")),
            ("yev", p.n("yev")), ("ydma", N_YDMA), ("gx", p.n("gx")),
            ("xdtr", p.n("xdtr")), ("hml", p.n("hml")), ("dbg_s", p.n("dbg_s"))]
    esB.close()
    esW.close()
    esP.close()
    esC = ExitStack()
    ckvb = S(esC, "ckvb", [128, T], BF16)
    kaug = S(esC, "kaug", [128, T], BF16)
    vaug = S(esC, "vaug", [128, 64 * 260], BF16)
    qpair = S(esC, "qpair", [128, 16 * 2 * 512], BF16)
    rope32 = S(esC, "rope32", [32, 16 * 512], BF16)
    xql = [S(esC, f"xql{i}", [128, 8 * 512], BF16) for i in range(2)]
    exps = [S(esC, f"exps{i}", [128, 512], BF16) for i in range(2)]
    expsm = S(esC, "expsm", [128, 512], BF16)
    recd = S(esC, "recd", [1, 512], F32)
    bcsb = S(esC, "bcsb", [64, 512], F32)
    acps = S(esC, "acps", [64, 512], BF16)
    dmask2 = S(esC, "dmask2", [128, 4 * 512], BF16)
    cq = S(esC, "cq", [16, 512], F32)
    sq = S(esC, "sq", [16, 512], F32)
    rt5a = S(esC, "rt5a", [16, 512], F32)
    rt5b = S(esC, "rt5b", [16, 512], F32)
    rt5c = S(esC, "rt5c", [16, 512], BF16)
    pk = esC.enter_context(nc.psum_tensor("pk", [64, 512], F32))
    pbc = esC.enter_context(nc.psum_tensor("pbc", [64, 512], F32))

    p.emit("sync", lambda e: e.dma_start(out=dmask2[:], in_=dmask_d[:]),
           waits=FENC, incs=[("c5", 16)])
    p.emit("gpsimd", lambda e: e.memset(vaug[:], 1.0), waits=FENC, incs=[("c5", 1)])
    N_C5 = p.n("c5")
    if maxphase < 5.12:
        esC.close()
        _finalize(nc, es, p, block_gate=True)
        return nc
    for kc16 in range(16):
        rq, cs = kc16 // 4, 512 * (kc16 % 4)
        w = [("ag_ck", 1)] + (FENC if kc16 == 0 else [])
        p.emit("sync", lambda e, kc16=kc16, rq=rq, cs=cs: e.dma_start(
            out=ckvb[:, 512 * kc16:512 * (kc16 + 1)],
            in_=ckv_ag[160 * rq:160 * rq + 128, cs:cs + 512]), waits=w, incs=[("ckl", 16)])
        p.emit("sync", lambda e, kc16=kc16, rq=rq, cs=cs: e.dma_start(
            out=kaug[64:80, 512 * kc16:512 * (kc16 + 1)],
            in_=ckv_ag[160 * rq + 128:160 * rq + 144, cs:cs + 512]), incs=[("ckl", 16)])
        p.emit("sync", lambda e, kc16=kc16, rq=rq, cs=cs: e.dma_start(
            out=kaug[80:96, 512 * kc16:512 * (kc16 + 1)],
            in_=ckv_ag[160 * rq + 144:160 * rq + 160, cs:cs + 512]), incs=[("ckl", 16)])
    N_CKL = p.n("ckl")
    if maxphase < 5.15:
        esC.close()
        _finalize(nc, es, p, block_gate=True)
        return nc
    for kc in range(64):
        w = [("ckl", N_CKL), ("wload", 64), ("c5", N_C5)] if kc == 0 else \
            [("vdv", kc)]
        p.emit("tensor", lambda e, kc=kc: e.matmul(
            pf[4][0:128, 0:256], ckvb[:, 128 * kc:128 * (kc + 1)], wvup4[:],
            start=True, stop=True), waits=w, incs=[("vdp", 1)])
        p.emit("vector", lambda e, kc=kc: (GV(e).tensor_copy(
            vaug[:, 260 * kc:260 * kc + 260].rearrange("q (h z) -> q h z", z=65)[:, :, 0:64],
            pf[4][0:128, 0:256].rearrange("q (h z) -> q h z", z=64)), None)[1],
            waits=[("vdp", kc + 1)], incs=[("vdv", 1)])
    N_VDV = p.n("vdv")

    if maxphase < 5.2:
        esC.close()
        _finalize(nc, es, p, block_gate=True)
        return nc

    for p01 in range(2):
        qpe_hist = []
        for qc in range(16):
            rq, cs = qc // 4, 512 * (qc % 4)
            w = [("ag_xd", 1)]
            if p01 == 0 and qc == 0:
                w += FENC
            nqpe_hist = p.n("qpe")
            if qc >= 2:
                w.append(("qpe", qpe_hist[qc - 2]))
            elif p01 == 1 and qc < 2:
                w.append(("qpe", p.n("qpe")))
            p.emit("sync", lambda e, qc=qc, rq=rq, cs=cs: e.dma_start(
                out=xql[qc % 2][:].rearrange("q (a t) -> q a t", a=8),
                in_=xd_ag[1024 * rq:1024 * (rq + 1), cs:cs + 512]
                    .rearrange("(a q) t -> q a t", q=128)),
                waits=w, incs=[("qld", 16)])
            NQLD = p.n("qld")
            for ci in range(8):
                w = []
                if ci == 0:
                    w = [("qld", NQLD), ("qv", p.n("qv"))]
                    if p01 == 0 and qc == 0:
                        w.append(("wload", 16))
                p.emit("tensor", lambda e, qc=qc, ci=ci, p01=p01: e.matmul(
                    pf[5][:], wq4[:, 256 * ci + 128 * p01:256 * ci + 128 * (p01 + 1)],
                    xql[qc % 2][:, 512 * ci:512 * (ci + 1)],
                    start=(ci == 0), stop=(ci == 7)),
                    waits=w,
                    incs=([("qpe", 1)] if (ci == 7 and (p01 == 1 or maxphase < 5.25)) else []))
            if p01 == 0 and maxphase >= 5.25:
                for half in range(2):
                    pfr = pf[4] if half == 0 else pf[3]
                    for ci in range(8):
                        w = []
                        if ci == 0 and half == 0:
                            w = [("rp_v", p.n("rp_v")), ("vdv", N_VDV)]
                        p.emit("tensor", lambda e, ci=ci, half=half, qc=qc, pfr=pfr:
                               e.matmul(pfr[0:16, 0:512],
                                        wqr[:, 32 * ci + 16 * half:32 * ci + 16 * (half + 1)],
                                        xql[qc % 2][:, 512 * ci:512 * (ci + 1)],
                                        start=(ci == 0), stop=(ci == 7)),
                               waits=w,
                               incs=([("qpe", 1)] if (ci == 7 and half == 1) else []))

                p.emit("sync", lambda e, qc=qc: e.dma_start(
                    out=cq[:], in_=cos_full[:, 512 * qc:512 * (qc + 1)]),
                    waits=[("rp_v", p.n("rp_v"))], incs=[("cql", 16)])
                p.emit("sync", lambda e, qc=qc: e.dma_start(
                    out=sq[:], in_=sin_full[:, 512 * qc:512 * (qc + 1)]),
                    incs=[("cql", 16)])

                def rop5(e, qc=qc):
                    e = GV(e)
                    c0, c1 = 512 * qc, 512 * (qc + 1)
                    e.tensor_tensor(rt5a[:], pf[4][0:16, 0:512], cq[:], ALU.mult)
                    e.tensor_tensor(rt5b[:], pf[3][0:16, 0:512], sq[:], ALU.mult)
                    e.tensor_tensor(rope32[0:16, c0:c1], rt5a[:], rt5b[:], ALU.subtract)
                    e.tensor_tensor(rt5a[:], pf[3][0:16, 0:512], cq[:], ALU.mult)
                    e.tensor_tensor(rt5b[:], pf[4][0:16, 0:512], sq[:], ALU.mult)
                    e.tensor_tensor(rt5c[:], rt5a[:], rt5b[:], ALU.add)
                p.emit("vector", rop5, waits=[("qpe", p.n("qpe")), ("rp_d", p.n("rp_d")),
                                              ("cql", p.n("cql"))],
                       incs=[("rp_v", 1)])
                p.emit("sync", lambda e, qc=qc: e.dma_start(
                    out=rope32[16:32, 512 * qc:512 * (qc + 1)], in_=rt5c[:]),
                    waits=[("rp_v", p.n("rp_v"))], incs=[("rp_d", 16)])
            qpe_hist.append(p.n("qpe"))
            for s01 in range(2):
                w = [("qpe", qpe_hist[-1])]
                if p01 == 1 and qc == 0 and s01 == 0:
                    w.append(("sc_pe", p.n("sc_pe")))
                p.emit("vector", lambda e, qc=qc, s01=s01: (GV(e).tensor_copy(
                    qpair[0:64, 512 * (2 * qc + s01):512 * (2 * qc + s01) + 512],
                    pf[5][64 * s01:64 * (s01 + 1), :]), None)[1], waits=w, incs=[("qv", 1)])
                if maxphase >= 5.27:
                    wr_ = [("qv", p.n("qv"))]
                    if p01 == 0:
                        wr_.append(("rp_d", p.n("rp_d")))
                    else:
                        wr_.append(("rp_d", 16 * 16))
                    p.emit("vector", lambda e, qc=qc, s01=s01: (GV(e).tensor_copy(
                        qpair[64:96, 512 * (2 * qc + s01):512 * (2 * qc + s01) + 512],
                        rope32[:, 512 * qc:512 * (qc + 1)]), None)[1], waits=wr_, incs=[("qv", 1)])
        NQV = p.n("qv")
        if maxphase < 5.3:
            break
        for s01 in range(2):
            hl = 2 * p01 + s01
            for kc5 in range(16):
                w = [("kdv", p.n("kdv"))]
                if kc5 == 0:
                    w += [("ckl", N_CKL), ("wload", 48), ("sc_pe", p.n("sc_pe")),
                          ("vdv", N_VDV)]
                p.emit("tensor", lambda e, hl=hl, kc5=kc5: e.matmul(
                    pk[:], wkup4[:, 64 * hl:64 * (hl + 1)],
                    ckvb[:, 512 * kc5:512 * (kc5 + 1)], start=True, stop=True),
                    waits=w, incs=[("kdp", 1)])
                p.emit("vector", lambda e, kc5=kc5: (GV(e).tensor_copy(
                    kaug[0:64, 512 * kc5:512 * (kc5 + 1)], pk[:]), None)[1],
                    waits=[("kdp", p.n("kdp"))], incs=[("kdv", 1)])
            NKDV = p.n("kdv")
            if maxphase < 5.4:
                continue
            for qc in range(16):
                nkt = 4 * qc + 4
                qcol = 512 * (2 * qc + s01)
                for kt in range(nkt):
                    sbank = pf[kt % 2]
                    exn = p.n("ex_s")
                    w = [("ex_s", exn - 1 if kt >= 2 else exn)]
                    if kt == 0:
                        w += [("kdv", NKDV), ("qv", NQV), ("ep_v", p.n("ep_v"))]
                    p.emit("tensor", lambda e, kt=kt, qcol=qcol, sbank=sbank: e.matmul(
                        sbank[:], kaug[0:96, 128 * kt:128 * (kt + 1)],
                        qpair[0:96, qcol:qcol + 512], start=True, stop=True),
                        waits=w, incs=[("sc_pe", 1)])
                    avn = p.n("av_k")
                    wexp = [("sc_pe", p.n("sc_pe")), ("msets", 4)]
                    if maxphase >= 5.45:
                        wexp.append(("av_k", avn - 1 if kt >= 2 else avn))
                    p.emit("scalar", lambda e, kt=kt, sbank=sbank: (GS(e).activation(
                        exps[kt % 2][:], sbank[:], AF.Exp, bias=bias25[:], scale=ISCALE),
                        None)[1], waits=wexp, incs=[("ex_s", 1)])
                    if maxphase < 5.45:
                        continue
                    diag = kt >= 4 * qc
                    if diag:
                        p.emit("vector", lambda e, kt=kt, qc=qc: (GV(e).tensor_tensor(
                            expsm[:], exps[kt % 2][:],
                            dmask2[:, 512 * (kt - 4 * qc):512 * (kt - 4 * qc + 1)],
                            ALU.mult), None)[1],
                            waits=[("ex_s", p.n("ex_s")), ("av_k", p.n("av_k")),
                                   ("c5", N_C5)],
                            incs=[("mk_v", 1)])
                        rhs_t = expsm
                        wav = [("mk_v", p.n("mk_v"))]
                    else:
                        rhs_t = exps[kt % 2]
                        wav = [("ex_s", p.n("ex_s"))]
                    p.emit("tensor", lambda e, kt=kt, hl=hl, rhs_t=rhs_t, nkt=nkt: e.matmul(
                        pf[2][0:65, :], vaug[:, 260 * kt + 65 * hl:260 * kt + 65 * hl + 65],
                        rhs_t[:], start=(kt == 0), stop=(kt == nkt - 1)),
                        waits=wav, incs=[("av_k", 1)])
                if maxphase < 5.48:
                    continue
                p.emit("vector", lambda e: (GV(e).reciprocal(recd[:], pf[2][64:65, :]), None)[1],
                       waits=[("av_k", p.n("av_k"))], incs=[("ep_v", 1)])
                p.emit("tensor", lambda e: e.matmul(
                    pbc[:], onesf[:], recd[:], start=True, stop=True),
                    waits=[("ep_v", p.n("ep_v"))], incs=[("bc_pe", 1)])
                p.emit("vector", lambda e: (GV(e).tensor_copy(bcsb[:], pbc[:]), None)[1],
                       waits=[("bc_pe", p.n("bc_pe"))], incs=[("ep_v", 1)])
                p.emit("vector", lambda e: (GV(e).tensor_tensor(
                    acps[:], pf[2][0:64, :], bcsb[:], ALU.mult), None)[1],
                    waits=[("ep_v", p.n("ep_v")), ("ac_d", p.n("ac_d"))],
                    incs=[("ep_v", 1)])
                p.emit("sync", lambda e, hl=hl, qc=qc: e.dma_start(
                    out=ac_in[64 * hl:64 * (hl + 1), 512 * qc:512 * (qc + 1)],
                    in_=acps[:]),
                    waits=[("ep_v", p.n("ep_v"))], incs=[("ac_d", 16)])
            if maxphase < 5.5:
                break
        if maxphase < 5.8:
            break
    N_ACD = p.n("ac_d")
    if maxphase < 6 and maxphase != int(maxphase):
        esC.close()
        _finalize(nc, es, p, block_gate=True)
        return nc

    p.emit("gpsimd", lambda e: e.collective_compute(
        "AllGather", ALU.bypass, replica_groups=G4, ins=[ac_in.opt()], outs=[ac_ag.opt()]),
        waits=[("ac_d", N_ACD)], incs=[("ag_ac", 1)])
    if debug:
        p.emit("sync", lambda e: e.dma_start(out=dbg["dbg_ac"][:], in_=ac_in[:]),
               waits=[("ac_d", N_ACD)], incs=[("dbg_s", 16)])

    if maxphase < 6:
        esC.close()
        _finalize(nc, es, p, block_gate=True)
        return nc
    FEND = [("ep_v", p.n("ep_v")), ("ac_d", N_ACD), ("qv", p.n("qv")),
            ("sc_pe", p.n("sc_pe")), ("av_k", p.n("av_k")), ("ex_s", p.n("ex_s")),
            ("mk_v", p.n("mk_v")), ("kdv", p.n("kdv")), ("vdv", N_VDV),
            ("rp_v", p.n("rp_v")), ("qld", p.n("qld")), ("dbg_s", p.n("dbg_s"))]
    esC.close()
    esD = ExitStack()
    fw1sb = S(esD, "fw1sb", [128, 8 * DI], BF16)
    fw2sb = S(esD, "fw2sb", [128, 32 * D], BF16)
    hs2 = S(esD, "hs2", [128, 32 * 256], BF16)
    ys2 = S(esD, "ys2", [128, D], F32)

    wload_ch(fw1sb, fw1_d, D, DI, waits=FEND + [("ag_ac", 1), ("ag_y", 1)])
    N_FW1 = p.n("wload")
    wload(fw2sb.rearrange("q (a c) -> q a c", a=32),
          fw2_d.rearrange("(a q) c -> q a c", q=128))
    N_FW2 = p.n("wload")

    for tt8 in range(8):
        for ic in range(32):
            pfh = pf[ic % 2]
            hcnt = tt8 * 32 + ic
            for ci in range(8):
                w = []
                if ci == 0:
                    w = [("wload", N_FW1)] + (FEND if (tt8 == 0 and ic == 0) else [])
                    if hcnt >= 2:
                        w.append(("fgel", hcnt - 1))
                p.emit("tensor", lambda e, ci=ci, ic=ic, tt8=tt8, pfh=pfh: e.matmul(
                    pfh[0:128, 0:256],
                    fw1sb[:, 4096 * ci + 128 * ic:4096 * ci + 128 * (ic + 1)],
                    xnd[:, 2048 * ci + 256 * tt8:2048 * ci + 256 * (tt8 + 1)],
                    start=(ci == 0), stop=(ci == 7)),
                    waits=w, incs=([("fhm", 1)] if ci == 7 else []))
            w = [("fhm", hcnt + 1)]
            if tt8 >= 1 and ic == 0:
                w.append(("fym", 4 * tt8))
            p.emit("scalar", lambda e, ic=ic, pfh=pfh: (GS(e).activation(
                hs2[:, 256 * ic:256 * (ic + 1)], pfh[0:128, 0:256], AF.Gelu,
                bias=fb1t[:, ic:ic + 1]), None)[1], waits=w, incs=[("fgel", 1)])
        for ss in range(2):
            for dh in range(2):
                pfy = pf[2 + (2 * ss + dh) % 2]
                ycnt = tt8 * 4 + ss * 2 + dh
                for ic in range(32):
                    w = []
                    if ic == 0:
                        w = [("fgel", 32 * (tt8 + 1)), ("wload", N_FW2)]
                        if ycnt >= 2:
                            w.append(("fyv", ycnt - 1))
                    p.emit("tensor", lambda e, ss=ss, dh=dh, ic=ic, pfy=pfy: e.matmul(
                        pfy[:], hs2[:, 256 * ic + 128 * ss:256 * ic + 128 * (ss + 1)],
                        fw2sb[:, 1024 * ic + 512 * dh:1024 * ic + 512 * (dh + 1)],
                        start=(ic == 0), stop=False), waits=w)
                p.emit("tensor", lambda e, dh=dh, pfy=pfy: e.matmul(
                    pfy[:], onesb[0:1, 0:128], fb2r[0:1, 512 * dh:512 * (dh + 1)],
                    start=False, stop=True),
                    waits=([("wload", 128)] if ycnt == 0 else []), incs=[("fym", 1)])
                p.emit("vector", lambda e, dh=dh, pfy=pfy: (GV(e).tensor_copy(
                    ys2[:, 512 * dh:512 * (dh + 1)], pfy[:]), None)[1],
                    waits=[("fym", ycnt + 1), ("fyd", 16 * (tt8 * 2 + ss))],
                    incs=[("fyv", 1)])
            p.emit("sync", lambda e, tt8=tt8, ss=ss: e.dma_start(
                out=yfb_d[256 * tt8 + 128 * ss:256 * tt8 + 128 * (ss + 1), :], in_=ys2[:]),
                waits=[("fyv", tt8 * 4 + 2 * (ss + 1))], incs=[("fyd", 16)])
    N_FYD = p.n("fyd")
    if debug:
        p.emit("sync", lambda e: e.dma_start(out=dbg["dbg_yfb"][:], in_=yfb_d[:]),
               waits=[("fyd", N_FYD)], incs=[("dbg_s", 16)])

    if maxphase < 7:
        esD.close()
        _finalize(nc, es, p, block_gate=True)
        return nc
    FENE = [("fym", p.n("fym")), ("fgel", p.n("fgel")), ("fyv", p.n("fyv")),
            ("fyd", N_FYD), ("fhm", p.n("fhm")), ("dbg_s", p.n("dbg_s"))]
    esD.close()
    esE = ExitStack()
    wosb = S(esE, "wosb", [128, 8 * D], BF16)
    woact = S(esE, "woact", [128, 8 * OWN], BF16)
    ga = [S(esE, f"ga{i}", [128, D], F32) for i in range(2)]
    gb = [S(esE, f"gb{i}", [128, D], F32) for i in range(2)]
    yf2 = [S(esE, f"yf2{i}", [128, D], F32) for i in range(2)]
    xs2 = [S(esE, f"xs2{i}", [128, D], F32) for i in range(2)]
    cmb = S(esE, "cmb", [128, D], F32)
    cmt = S(esE, "cmt", [128, D], F32)
    outsb = [S(esE, f"outsb{i}", [128, D], F32) for i in range(2)]

    wload_ch(wosb, wo_d, D, D, waits=FENE + [("ag_ac", 1)])
    N_WO = p.n("wload")
    for blk in range(8):
        p.emit("sync", lambda e, blk=blk: e.dma_start(
            out=woact[:, 2048 * blk:2048 * (blk + 1)],
            in_=ac_ag[128 * blk:128 * (blk + 1), ds(ctx["rk4"] * 2048, 2048)]),
            waits=[("ag_ac", 1)] + (FENE if blk == 0 else []), incs=[("wol", 16)])
    N_WOL = p.n("wol")

    p.emit("gpsimd", lambda e: (ctx.__setitem__("bcy", e.to_reg(NE * SLOTS - 1)), e.nop())[1],
           waits=FENE, incs=[("bcr2", 1)])

    for k in range(16):
        bk = k % 2
        w = FENE + [("msets", 4)] if k == 0 else []
        if k >= 2:
            w = [("cmv", 3 * (k - 1))]
        p.emit("vector", lambda e, bk=bk: (GV(e).memset(ga[bk][:], 0.0),
                                           GV(e).memset(gb[bk][:], 0.0), None)[2],
               waits=w, incs=[("cmz", 1)])
        p.emit("gpsimd", lambda e, k=k, bk=bk: e.indirect_dma_start(
            out=ga[bk][:], out_offset=None, in_=y_ag[:],
            in_offset=IndirectOffsetOnAxis(ap=offa_o[:, k:k + 1], axis=0),
            bounds_check=ctx["bcy"], oob_is_err=False),
            waits=[("cmz", k + 1), ("ag_y", 1), ("ag_ac", 1), ("own_ld", N_OWNLD), ("bcr2", 1)],
            incs=[("cga", 16)])
        p.emit("gpsimd", lambda e, k=k, bk=bk: e.indirect_dma_start(
            out=gb[bk][:], out_offset=None, in_=y_ag[:],
            in_offset=IndirectOffsetOnAxis(ap=offb_o[:, k:k + 1], axis=0),
            bounds_check=ctx["bcy"], oob_is_err=False), incs=[("cga", 16)])
        wx = [("cmv", 3 * (k - 1))] if k >= 2 else []
        p.emit("sync", lambda e, k=k, bk=bk: e.dma_start(
            out=xs2[bk][:], in_=x_own[128 * k:128 * (k + 1), :]),
            waits=wx + ([("fyd", N_FYD)] if k == 0 else []), incs=[("xld2", 16)])
        p.emit("sync", lambda e, k=k, bk=bk: e.dma_start(
            out=yf2[bk][:], in_=yfb_d[128 * k:128 * (k + 1), :]), incs=[("xld2", 16)])
        for dh in range(2):
            for blk in range(8):
                w = []
                if blk == 0:
                    w = [("wol", N_WOL)]
                    if k == 0 and dh == 0:
                        w.append(("wload", N_WO))
                    if k >= 1:
                        w.append(("cmv", 3 * k))
                p.emit("tensor", lambda e, k=k, dh=dh, blk=blk: e.matmul(
                    pf[dh][:], woact[:, 2048 * blk + 128 * k:2048 * blk + 128 * (k + 1)],
                    wosb[:, 1024 * blk + 512 * dh:1024 * blk + 512 * (dh + 1)],
                    start=(blk == 0), stop=(blk == 7)),
                    waits=w, incs=([("wop", 1)] if blk == 7 else []))

        def comb(e, k=k, bk=bk):
            e = GV(e)
            e.tensor_scalar(cmb[:], ga[bk][:], wab_o[:, k:k + 1], None, ALU.mult)
            e.tensor_scalar(cmt[:], gb[bk][:], wab_o[:, 16 + k:16 + k + 1], None, ALU.mult)
            e.tensor_tensor(cmb[:], cmb[:], cmt[:], ALU.add)
            e.tensor_scalar(cmt[:], yf2[bk][:], wab_o[:, 32 + k:32 + k + 1], None, ALU.mult)
            e.tensor_tensor(cmb[:], cmb[:], cmt[:], ALU.add)
            e.tensor_tensor(cmb[:], cmb[:], xs2[bk][:], ALU.add)
        p.emit("vector", comb,
               waits=[("cga", 32 * (k + 1)), ("xld2", 32 * (k + 1)),
                      ("own_ld", N_OWNLD)], incs=[("cmv", 1)])
        for dh in range(2):
            wev = [("wop", 2 * (k + 1)), ("cmv", 3 * k + 1)]
            if k >= 2:
                wev.append(("stout", 16 * (k - 1)))
            p.emit("vector", lambda e, dh=dh, bk=bk: (GV(e).tensor_tensor(
                outsb[bk][:, 512 * dh:512 * (dh + 1)], pf[dh][:],
                cmb[:, 512 * dh:512 * (dh + 1)], ALU.add), None)[1],
                waits=wev, incs=[("cmv", 1)])
        p.emit("sync", lambda e, k=k, bk=bk: e.dma_start(
            out=out_d[128 * k:128 * (k + 1), :], in_=outsb[bk][:]),
            waits=[("cmv", 3 * (k + 1))], incs=[("stout", 16)])
    p.wait("sync", "stout", 16 * 16)

    esE.close()
    _finalize(nc, es, p)
    return nc


def _finalize(nc, es, p, block_gate=False):
    block = es.enter_context(nc.Block())

    @block.sync
    def _(sync):
        for f in p.q["sync"]:
            f(sync)

    @block.tensor
    def _(tensor):
        for f in p.q["tensor"]:
            f(tensor)

    @block.vector
    def _(vector):
        for f in p.q["vector"]:
            f(vector)

    @block.scalar
    def _(scalar):
        for f in p.q["scalar"]:
            f(scalar)

    @block.gpsimd
    def _(gpsimd):
        for f in p.q["gpsimd"]:
            f(gpsimd)

    es.close()


_BF = ml_dtypes.bfloat16


def _rope_tables():
    inv_freq = (1.0 / (10000.0 ** (np.arange(0, DR, 2, dtype=np.float32) / DR))
                ).astype(np.float32)
    t = np.arange(T, dtype=np.float32)
    freqs = t[:, None] * inv_freq[None, :]
    return np.cos(freqs).T.copy(), np.sin(freqs).T.copy()


def _consts():
    identb = np.eye(128, dtype=_BF)
    identf = np.eye(128, dtype=np.float32)
    uexcl = (np.arange(128)[:, None] < np.arange(128)[None, :]).astype(_BF)
    kk = np.arange(128)[:, None]
    qq = np.arange(512)[None, :]
    dmask = np.concatenate(
        [(kk <= qq - 128 * j).astype(_BF) for j in range(4)], axis=1)
    return identb, identf, uexcl, dmask


def kernel(L0_attn_x, L0_attn_W_kv_down, L0_attn_W_k_up, L0_attn_W_v_up, L0_attn_W_q,
           L0_attn_W_q_rope, L0_attn_W_k_rope, L0_attn_W_o, L0_ffn_Wr, L0_ffn_W1,
           L0_ffn_b1, L0_ffn_W2, L0_ffn_b2, L0_ffn_fw1, L0_ffn_fb1, L0_ffn_fw2,
           L0_ffn_fb2, _debug=0, _maxphase=7, _p4st=20):
    x = np.ascontiguousarray(np.asarray(L0_attn_x, dtype=np.float32))
    cosT, sinT = _rope_tables()
    identb, identf, uexcl, dmask = _consts()
    f32 = lambda a: np.ascontiguousarray(np.asarray(a, dtype=np.float32))
    Wq, Wkv = f32(L0_attn_W_q), f32(L0_attn_W_kv_down)
    Wkup, Wvup = f32(L0_attn_W_k_up), f32(L0_attn_W_v_up)
    Wqr, Wkr, Wo = f32(L0_attn_W_q_rope), f32(L0_attn_W_k_rope), f32(L0_attn_W_o)
    Wr, W1, b1 = f32(L0_ffn_Wr), f32(L0_ffn_W1), f32(L0_ffn_b1)
    W2, b2 = f32(L0_ffn_W2), f32(L0_ffn_b2)
    fw1, fb1, fw2, fb2 = f32(L0_ffn_fw1), f32(L0_ffn_fb1), f32(L0_ffn_fw2), f32(L0_ffn_fb2)

    nc = build(debug=_debug, maxphase=_maxphase, p4st=_p4st)
    in_maps = []
    for c in range(8):
        b, wq_ = c // 4, c % 4
        oh = np.zeros((128, NE), np.float32)
        oh[:, c] = 1.0
        in_maps.append({
            "x_own": x[b, 2048 * wq_:2048 * (wq_ + 1)].copy(),
            "cos_own": cosT[:, 2048 * wq_:2048 * (wq_ + 1)].copy(),
            "sin_own": sinT[:, 2048 * wq_:2048 * (wq_ + 1)].copy(),
            "cos_full": cosT, "sin_full": sinT,
            "wq4": Wq[:, 256 * wq_:256 * (wq_ + 1)].copy(),
            "wkv": Wkv, "wkup4": Wkup[:, 256 * wq_:256 * (wq_ + 1)].copy(),
            "wvup4": Wvup[:, 256 * wq_:256 * (wq_ + 1)].copy(),
            "wqr": Wqr, "wkr": Wkr, "wo": Wo, "wr": Wr,
            "w1e": W1[c], "b1e": b1[c], "w2e": W2[c], "b2e": b2[c],
            "fw1": fw1, "fb1": fb1, "fw2": fw2, "fb2": fb2,
            "identb": identb, "identf": identf, "uexcl": uexcl, "dmask": dmask,
            "ohsel": oh,
        })
    res = run_bass_kernel_spmd(nc, in_maps, core_ids=list(range(8)))
    out = np.empty((B, T, D), np.float32)
    for c in range(8):
        b, wq_ = c // 4, c % 4
        out[b, 2048 * wq_:2048 * (wq_ + 1)] = np.asarray(res.results[c]["out_own"])
    if _debug:
        kernel._dbg = res
    return out


# version 05
# speedup vs baseline: 1.0545x; 1.0545x over previous
import math
import numpy as np
import ml_dtypes
from contextlib import ExitStack

import concourse.bass as bass
import concourse.mybir as mybir
from concourse.bass import IndirectOffsetOnAxis
from concourse.bass_utils import run_bass_kernel_spmd

F32 = mybir.dt.float32
BF16 = mybir.dt.bfloat16
U32 = mybir.dt.uint32
AF = mybir.ActivationFunctionType
ALU = mybir.AluOpType
AX = mybir.AxisListType
ds = bass.ds

B, T, D = 2, 8192, 1024
NH, HD, DC, DR = 16, 64, 128, 32
NE, CAP, DI = 8, 5120, 4096
OWN = 2048
NTILE = OWN // 128
TT = B * T
SLOTS = CAP
NSC = SLOTS // 128
ISCALE = 1.0 / math.sqrt(HD + DR)
SHIFT = -25.0
OORF = 1.0e6
ENGS = ("sync", "tensor", "vector", "scalar", "gpsimd")
G8 = [list(range(8))]
G4 = [[0, 1, 2, 3], [4, 5, 6, 7]]


class Prog:
    def __init__(self, nc, es):
        self.nc, self.es = nc, es
        self.q = {e: [] for e in ENGS}
        self.sems = {}

    def sem(self, name):
        if name not in self.sems:
            self.sems[name] = [self.es.enter_context(self.nc.semaphore("s_" + name)), 0]
        return self.sems[name]

    def n(self, name):
        return self.sems[name][1] if name in self.sems else 0

    def emit(self, eng, fn, waits=(), incs=()):
        wl = [(self.sem(nm)[0], v) for nm, v in waits]
        il = []
        for nm, k in incs:
            s = self.sem(nm)
            il.append((s[0], k))
            s[1] += k

        def g(e):
            for h, v in wl:
                e.wait_ge(h, v)
            ins = fn(e)
            if il:
                if ins is None:
                    ins = e.nop()
                for h, k in il:
                    ins.then_inc(h, k)
        self.q[eng].append(g)

    def bump(self, nm, k):
        s = self.sem(nm)
        s[1] += k
        return s[0]

    def wait(self, eng, nm, v=None):
        v = self.n(nm) if v is None else v
        h = self.sem(nm)[0]
        self.q[eng].append(lambda e, h=h, v=v: e.wait_ge(h, v))


def build(debug=0, maxphase=7, p4st=20):
    nc = bass.Bass("TRN2", target_bir_lowering=False)
    es = ExitStack()

    def din(name, shape, dt=F32):
        return nc.dram_tensor(name, shape, dt, kind="ExternalInput").ap()

    def dout(name, shape, dt=F32):
        return nc.dram_tensor(name, shape, dt, kind="ExternalOutput").ap()

    def dint(name, shape, dt):
        return nc.dram_tensor(name, shape, dt).ap()

    x_own = din("x_own", [OWN, D])
    cos_own = din("cos_own", [16, OWN])
    sin_own = din("sin_own", [16, OWN])
    cos_full = din("cos_full", [16, T])
    sin_full = din("sin_full", [16, T])
    wq4_d = din("wq4", [D, 256])
    wkv_d = din("wkv", [D, DC])
    wkup4_d = din("wkup4", [DC, 256])
    wvup4_d = din("wvup4", [DC, 256])
    wqr_d = din("wqr", [D, DR])
    wkr_d = din("wkr", [DC, DR])
    wo_d = din("wo", [D, D])
    wr_d = din("wr", [NE, D])
    w1_d = din("w1e", [D, DI])
    b1_d = din("b1e", [DI])
    w2_d = din("w2e", [DI, D])
    b2_d = din("b2e", [D])
    fw1_d = din("fw1", [D, DI])
    fb1_d = din("fb1", [DI])
    fw2_d = din("fw2", [DI, D])
    fb2_d = din("fb2", [D])
    identb_d = din("identb", [128, 128], BF16)
    identf_d = din("identf", [128, 128], F32)
    uexcl_d = din("uexcl", [128, 128], BF16)
    dmask_d = din("dmask", [128, 4 * 512], BF16)
    ohsel_d = din("ohsel", [128, NE])

    out_d = dout("out_own", [OWN, D], F32)

    lg_in = dint("lg_in", [NE, OWN], F32)
    lg_ag = dint("lg_ag", [NE * 8, OWN], F32)
    ckv_in = dint("ckv_in", [160, OWN], BF16)
    ckv_ag = dint("ckv_ag", [640, OWN], BF16)
    xtok_in = dint("xtok_in", [OWN, D], BF16)
    xtok_ag = nc.dram_tensor("xtok_ag", [TT, D], BF16, addr_space="Shared").ap()
    xd_in = dint("xd_in", [D, OWN], BF16)
    xd_ag = dint("xd_ag", [4 * D, OWN], BF16)
    ac_in = dint("ac_in", [256, T], BF16)
    ac_ag = dint("ac_ag", [1024, T], BF16)
    y_in = dint("y_in", [SLOTS, D], F32)
    y_ag = nc.dram_tensor("y_ag", [NE * SLOTS, D], F32, addr_space="Shared").ap()
    inv_dd = dint("inv_dd", [SLOTS, 1], U32)
    yfb_d = dint("yfb_d", [OWN, D], F32)
    offa_d = dint("offa_d", [128, 128], U32)
    offb_d = dint("offb_d", [128, 128], U32)
    wab_d = dint("wab_d", [128, 3 * 128], F32)

    dbg = {}
    if debug:
        dbg["dbg_xn"] = dout("dbg_xn", [OWN, D], BF16)
        dbg["dbg_pos"] = dout("dbg_pos", [128, 128], F32)
        dbg["dbg_kept"] = dout("dbg_kept", [128, 128], F32)
        dbg["dbg_fb"] = dout("dbg_fb", [128, 128], F32)
        dbg["dbg_w8"] = dout("dbg_w8", [128, NE * 128], F32)
        dbg["dbg_offa"] = dout("dbg_offa", [128, 128], U32)
        dbg["dbg_inv"] = dout("dbg_inv", [128, NSC], U32)
        dbg["dbg_y"] = dout("dbg_y", [SLOTS, D], F32)
        dbg["dbg_yfb"] = dout("dbg_yfb", [OWN, D], F32)
        dbg["dbg_ac"] = dout("dbg_ac", [256, T], BF16)

    def S(st, name, shape, dt):
        side = "left" if (st is es or st is esW) else "right"
        return st.enter_context(nc.sbuf_tensor(name, shape, dt, side=side))

    identb = S(es, "identb_s", [128, 128], BF16)
    identf = S(es, "identf_s", [128, 128], F32)
    uexcl = S(es, "uexcl_s", [128, 128], BF16)
    ohsel = S(es, "ohsel_s", [128, NE], F32)
    onesb = S(es, "onesb", [1, 128], BF16)
    onesf = S(es, "onesf", [1, 64], F32)
    bias25 = S(es, "bias25", [128, 1], F32)
    ceps = S(es, "ceps", [128, 1], F32)
    wq4 = S(es, "wq4_s", [128, 8 * 256], BF16)
    wkv = S(es, "wkv_s", [128, 8 * DC], BF16)
    wkup4 = S(es, "wkup4_s", [128, 256], BF16)
    wvup4 = S(es, "wvup4_s", [128, 256], BF16)
    wqr = S(es, "wqr_s", [128, 8 * DR], BF16)
    wkr = S(es, "wkr_s", [128, DR], BF16)
    wrt = S(es, "wrt", [128, 64], BF16)
    b1t = S(es, "b1t", [128, 32], F32)
    b2r = S(es, "b2r", [1, D], BF16)
    fb1t = S(es, "fb1t", [128, 32], F32)
    fb2r = S(es, "fb2r", [1, D], BF16)
    xnd = S(es, "xnd", [128, 8 * OWN], BF16)
    osct = S(es, "osct", [128, 128], U32)
    invs = S(es, "invs", [128, NSC], U32)
    offa_o = S(es, "offa_o", [128, 16], U32)
    offb_o = S(es, "offb_o", [128, 16], U32)
    wab_o = S(es, "wab_o", [128, 3 * 16], F32)
    trf = S(es, "trf", [128, 128], F32)
    trfu = S(es, "trfu", [128, 128], U32)
    def PS(name, shape, dt):
        return es.enter_context(nc.psum_tensor(name, shape, dt))

    pf = [PS(f"pf{i}", [128, 512], F32) for i in range(6)]
    esP = ExitStack()
    pb0 = esP.enter_context(nc.psum_tensor("pb0", [128, 256], BF16))
    pb1 = esP.enter_context(nc.psum_tensor("pb1", [128, 256], BF16))

    esW = ExitStack()
    w1sb = S(esW, "w1sb", [128, 8 * DI], BF16)

    p = Prog(nc, es)
    ctx = {}
    chv_h = p.sem("chv")[0]
    chs_h = p.sem("chs")[0]
    chv_n = [0]
    chs_n = [0]

    class _G:
        def __init__(self, e, h, n):
            self._e, self._h, self._n = e, h, n

        def __getattr__(self, nm):
            f = getattr(self._e, nm)
            if nm in ("wait_ge", "cc_rank", "to_reg", "nop"):
                return f

            def w(*a, **k):
                ins = f(*a, **k)
                ins.then_inc(self._h, 1)
                self._n[0] += 1
                self._e.wait_ge(self._h, self._n[0])
                return ins
            return w

    def GV(e):
        return _G(e, chv_h, chv_n)

    def GS(e):
        return _G(e, chs_h, chs_n)

    esA = ExitStack()
    xs = [S(esA, f"xs{i}", [128, D], F32) for i in range(2)]
    sqscr = S(esA, "sqscr", [128, D], BF16)
    st_sum = S(esA, "st_sum", [128, 1], F32)
    st_sq = S(esA, "st_sq", [128, 1], F32)
    st_nm = S(esA, "st_nm", [128, 1], F32)
    st_var = S(esA, "st_var", [128, 1], F32)
    st_t1 = S(esA, "st_t1", [128, 1], F32)
    st_rs = S(esA, "st_rs", [128, 1], F32)
    xntok = [S(esA, f"xntok{i}", [128, D], BF16) for i in range(2)]
    ckvo = S(esA, "ckvo", [128, OWN], BF16)
    kr1 = S(esA, "kr1", [16, OWN], BF16)
    kr2 = S(esA, "kr2", [16, OWN], BF16)
    lgsb = S(esA, "lgsb", [8, OWN], F32)
    cosn = S(esA, "cosn", [16, OWN], F32)
    sinn = S(esA, "sinn", [16, OWN], F32)
    rta = S(esA, "rta", [16, 128], F32)
    rtb = S(esA, "rtb", [16, 128], F32)
    wrf = S(esA, "wrf", [8, D], F32)
    b1f = S(esA, "b1f", [32, 128], F32)
    pl_l = S(esA, "pl_l", [128, NE * 128], F32)
    pl_m = S(esA, "pl_m", [128, 128], F32)
    pl_t = S(esA, "pl_t", [128, 128], F32)
    pl_u = S(esA, "pl_u", [128, 128], F32)
    pl_x = S(esA, "pl_x", [128, 128], F32)
    pl_f1 = S(esA, "pl_f1", [128, 128], F32)
    pl_f2 = S(esA, "pl_f2", [128, 128], F32)
    pl_m1 = S(esA, "pl_m1", [128, NE * 128], F32)
    pl_m2 = S(esA, "pl_m2", [128, NE * 128], F32)
    pl_w8 = S(esA, "pl_w8", [128, NE * 128], F32)
    pl_fb = S(esA, "pl_fb", [128, 128], F32)
    pl_cnt = S(esA, "pl_cnt", [128, 128], F32)
    pl_offa = S(esA, "pl_offa", [128, 128], F32)
    pl_offb = S(esA, "pl_offb", [128, 128], F32)
    pl_wa = S(esA, "pl_wa", [128, 128], F32)
    pl_wb = S(esA, "pl_wb", [128, 128], F32)
    pl_posmy = S(esA, "pl_posmy", [128, 128], F32)
    pl_keptmy = S(esA, "pl_keptmy", [128, 128], F32)
    pl_elig = S(esA, "pl_elig", [128, 128], F32)
    pl_scan = S(esA, "pl_scan", [128, 128], F32)
    pl_bs = S(esA, "pl_bs", [128, 1], BF16)
    pl_excl = S(esA, "pl_excl", [128, 1], F32)
    pl_pos = S(esA, "pl_pos", [128, 128], F32)
    pl_kept = S(esA, "pl_kept", [128, 128], F32)
    pl_sel = S(esA, "pl_sel", [128, 128], F32)
    pl_osc = S(esA, "pl_osc", [128, 128], F32)
    payj = S(esA, "payj", [128, 128], U32)
    pl_iu = S(esA, "pl_iu", [128, 128], U32)
    pl_iv = S(esA, "pl_iv", [128, 128], U32)
    zrow = S(esA, "zrow", [128, NSC], U32)

    for dst, src in [(identb, identb_d), (identf, identf_d), (uexcl, uexcl_d),
                     (ohsel, ohsel_d), (cosn, cos_own), (sinn, sin_own),
                     (wrf, wr_d)]:
        p.emit("sync", lambda e, d=dst, s=src: e.dma_start(out=d[:], in_=s[:]),
               incs=[("consts", 16)])
    N_CD = p.n("consts")

    for tile, val in [(onesb, 1.0), (onesf, 1.0), (bias25, SHIFT), (ceps, 1e-5)]:
        p.emit("vector", lambda e, t=tile, v=val: (GV(e).memset(t[:], v), None)[1],
               incs=[("msets", 1)])
    for tile, val in [(pl_fb, 0.0), (pl_cnt, 0.0), (pl_offa, OORF), (pl_offb, OORF),
                      (pl_wa, 0.0), (pl_wb, 0.0), (pl_posmy, 0.0), (pl_keptmy, 0.0),
                      (pl_f1, 0.0), (pl_f2, 0.0)]:
        p.emit("gpsimd", lambda e, t=tile, v=val: e.memset(t[:], v), incs=[("msets2", 1)])
    p.emit("gpsimd", lambda e: e.memset(zrow[:], 0), incs=[("msets2", 1)])
    N_MS2 = p.n("msets2")

    def wload(dst_ap, src_ap, waits=()):
        p.emit("gpsimd", lambda e, d=dst_ap, s=src_ap: e.dma_start(out=d, in_=s),
               waits=waits, incs=[("wload", 16)])

    def wload_ch(dst, src, rows, cols, waits=()):
        a = rows // 128
        wload(dst[:, 0:a * cols].rearrange("q (a c) -> q a c", a=a),
              src.rearrange("(a q) c -> q a c", q=128), waits=waits)

    wload_ch(wq4, wq4_d, D, 256)
    wload_ch(wkv, wkv_d, D, DC)
    wload(wkup4[:], wkup4_d[:])
    wload(wvup4[:], wvup4_d[:])
    wload_ch(wqr, wqr_d, D, DR)
    wload(wkr[:], wkr_d[:])
    wload(b2r[:], b2_d[None, :])
    wload(fb2r[:], fb2_d[None, :])
    N_WSMALL = p.n("wload")
    wload_ch(w1sb, w1_d, D, DI)
    N_W1 = p.n("wload")
    p.emit("gpsimd", lambda e: e.dma_start(
        out=inv_dd.rearrange("(b a) c -> b (a c)", b=128), in_=zrow[:]),
        waits=[("msets2", N_MS2)], incs=[("inv0", 16)])

    for ci in range(8):
        p.emit("tensor", lambda e, ci=ci: e.transpose(
            pf[5][0:128, 0:8], wrf[0:8, 128 * ci:128 * (ci + 1)], identf[0:8, 0:8]),
            waits=[("consts", N_CD)] + ([("wrt_v", ci)] if ci else []),
            incs=[("wrt_pe", 1)])
        p.emit("vector", lambda e, ci=ci: (GV(e).tensor_copy(
            wrt[:, 8 * ci:8 * (ci + 1)], pf[5][0:128, 0:8]), None)[1],
            waits=[("wrt_pe", ci + 1)], incs=[("wrt_v", 1)])
    for bsrc, bdst in [(b1_d, b1t), (fb1_d, fb1t)]:
        p.emit("sync", lambda e, s=bsrc: e.dma_start(
            out=b1f[:], in_=s.rearrange("(a c) -> a c", c=128)),
            waits=[("bT_pe", p.n("bT_pe"))], incs=[("bT_ld", 16)])
        p.emit("tensor", lambda e: e.transpose(pf[5][0:128, 0:32], b1f[:], identf[0:32, 0:32]),
               waits=[("bT_ld", p.n("bT_ld")), ("wrt_v", 8)], incs=[("bT_pe", 1)])
        p.emit("vector", lambda e, d=bdst: (GV(e).tensor_copy(d[:], pf[5][0:128, 0:32]), None)[1],
               waits=[("bT_pe", p.n("bT_pe"))], incs=[("bT_v", 1)])
    N_BT = p.n("bT_v")

    for i in range(NTILE):
        bi = i % 2
        w = [("ln_v", 3 * (i - 1))] if i >= 2 else []
        p.emit("sync", lambda e, i=i, bi=bi: e.dma_start(
            out=xs[bi][:], in_=x_own[128 * i:128 * (i + 1), :]), waits=w,
            incs=[("x_ld", 16)])
        p.emit("scalar", lambda e, bi=bi: (GS(e).activation(
            sqscr[:], xs[bi][:], AF.Square, accum_out=st_sq[:]), None)[1],
            waits=[("x_ld", 16 * (i + 1))] + ([("ln_v", 3 * i)] if i else []),
            incs=[("ln_s", 1)])
        p.emit("vector", lambda e, bi=bi: (GV(e).reduce_sum(st_sum[:], xs[bi][:], axis=AX.X), None)[1],
               waits=[("x_ld", 16 * (i + 1))] + ([("ln_s", 3 * i)] if i else []),
               incs=[("ln_v", 1)])

        def stats(e):
            e = GV(e)
            e.tensor_scalar_mul(st_nm[:], st_sum[:], -1.0 / D)
            e.tensor_tensor(st_t1[:], st_nm[:], st_nm[:], ALU.mult)
            e.tensor_scalar_mul(st_var[:], st_sq[:], 1.0 / D)
            e.tensor_tensor(st_var[:], st_var[:], st_t1[:], ALU.subtract)
        p.emit("vector", stats, waits=[("ln_s", 3 * i + 1)], incs=[("ln_v", 1)])

        def rstd(e):
            e = GS(e)
            e.activation(st_t1[:], st_var[:], AF.Ln, bias=ceps[:])
            e.activation(st_rs[:], st_t1[:], AF.Exp, scale=-0.5)
        p.emit("scalar", rstd, waits=[("ln_v", 3 * i + 2), ("msets", 4)],
               incs=[("ln_s", 2)])
        wxn = [("ln_s", 3 * i + 3)]
        if i >= 2:
            wxn += [("xn_dma", 16 * (i - 1)), ("tr_pe", 8 * (i - 1))]
        p.emit("vector", lambda e, bi=bi: e.tensor_scalar(
            xntok[bi][:], xs[bi][:], st_nm[:], st_rs[:], ALU.add, ALU.mult),
            waits=wxn, incs=[("ln_v", 1)])
        lnv = 3 * (i + 1)
        p.emit("sync", lambda e, i=i, bi=bi: e.dma_start(
            out=xtok_in[128 * i:128 * (i + 1), :], in_=xntok[bi][:]),
            waits=[("ln_v", lnv)], incs=[("xn_dma", 16)])
        if debug:
            p.emit("sync", lambda e, i=i, bi=bi: e.dma_start(
                out=dbg["dbg_xn"][128 * i:128 * (i + 1), :], in_=xntok[bi][:]),
                waits=[("ln_v", lnv)], incs=[("dbg_s", 16)])
        for ci in range(8):
            tcnt = i * 8 + ci
            pbx = pb0 if tcnt % 2 == 0 else pb1
            w = [("ln_v", lnv)]
            if tcnt >= 2:
                w.append(("tr_v", tcnt - 1))
            p.emit("tensor", lambda e, bi=bi, ci=ci, pbx=pbx: e.transpose(
                pbx[:, 0:128], xntok[bi][:, 128 * ci:128 * (ci + 1)], identb[:]),
                waits=w, incs=[("tr_pe", 1)])
            p.emit("vector", lambda e, i=i, ci=ci, pbx=pbx: (GV(e).tensor_copy(
                xnd[:, 2048 * ci + 128 * i:2048 * ci + 128 * (i + 1)], pbx[:, 0:128]), None)[1],
                waits=[("tr_pe", tcnt + 1)], incs=[("tr_v", 1)])
        trv = 8 * (i + 1)
        for ci in range(8):
            w = []
            if ci == 0:
                w = [("tr_v", trv), ("wrt_v", 8)]
                if i >= 1:
                    w.append(("proj_v", 2 * i))
            p.emit("tensor", lambda e, ci=ci, i=i: e.matmul(
                pf[4][0:8, 0:128], wrt[:, 8 * ci:8 * (ci + 1)],
                xnd[:, 2048 * ci + 128 * i:2048 * ci + 128 * (i + 1)],
                start=(ci == 0), stop=(ci == 7)),
                waits=w, incs=([("proj_pe", 1)] if ci == 7 else []))
        for ci in range(8):
            w = [("wload", 32)] if (ci == 0 and i == 0) else []
            p.emit("tensor", lambda e, ci=ci, i=i: e.matmul(
                pf[3][0:128, 0:128], wkv[:, 128 * ci:128 * (ci + 1)],
                xnd[:, 2048 * ci + 128 * i:2048 * ci + 128 * (i + 1)],
                start=(ci == 0), stop=(ci == 7)),
                waits=w, incs=([("proj_pe", 1)] if ci == 7 else []))
        p.emit("vector", lambda e, i=i: (GV(e).tensor_copy(
            lgsb[0:8, 128 * i:128 * (i + 1)], pf[4][0:8, 0:128]), None)[1],
            waits=[("proj_pe", 2 * i + 1)], incs=[("proj_v", 1)])
        p.emit("vector", lambda e, i=i: (GV(e).tensor_copy(
            ckvo[:, 128 * i:128 * (i + 1)], pf[3][0:128, 0:128]), None)[1],
            waits=[("proj_pe", 2 * i + 2)], incs=[("proj_v", 1)])
        for half in range(2):
            w = [("proj_v", 2 * (i + 1))]
            if i == 0 and half == 0:
                w.append(("wload", 96))
            if i >= 1 and half == 0:
                w.append(("kr_v", 6 * i))
            p.emit("tensor", lambda e, i=i, half=half: e.matmul(
                pf[half][0:16, 0:128], wkr[:, 16 * half:16 * (half + 1)],
                ckvo[:, 128 * i:128 * (i + 1)], start=True, stop=True),
                waits=w, incs=[("kr_pe", 1)])

        def kro(e, i=i):
            e = GV(e)
            c0, c1 = 128 * i, 128 * (i + 1)
            e.tensor_tensor(rta[:], pf[0][0:16, 0:128], cosn[:, c0:c1], ALU.mult)
            e.tensor_tensor(rtb[:], pf[1][0:16, 0:128], sinn[:, c0:c1], ALU.mult)
            e.tensor_tensor(kr1[:, c0:c1], rta[:], rtb[:], ALU.subtract)
            e.tensor_tensor(rta[:], pf[1][0:16, 0:128], cosn[:, c0:c1], ALU.mult)
            e.tensor_tensor(rtb[:], pf[0][0:16, 0:128], sinn[:, c0:c1], ALU.mult)
            e.tensor_tensor(kr2[:, c0:c1], rta[:], rtb[:], ALU.add)
        p.emit("vector", kro, waits=[("kr_pe", 2 * (i + 1)), ("consts", N_CD)],
               incs=[("kr_v", 6)])

    p.emit("sync", lambda e: e.dma_start(out=ckv_in[0:128, :], in_=ckvo[:]),
           waits=[("proj_v", 2 * NTILE)], incs=[("p1d", 16)])
    p.emit("sync", lambda e: e.dma_start(out=ckv_in[128:144, :], in_=kr1[:]),
           waits=[("kr_v", 6 * NTILE)], incs=[("p1d", 16)])
    p.emit("sync", lambda e: e.dma_start(out=ckv_in[144:160, :], in_=kr2[:]),
           incs=[("p1d", 16)])
    p.emit("sync", lambda e: e.dma_start(out=lg_in[:], in_=lgsb[:]),
           waits=[("proj_v", 2 * NTILE)], incs=[("p1d", 16)])
    p.emit("sync", lambda e: e.dma_start(
        out=xd_in.rearrange("(a q) t -> q a t", q=128),
        in_=xnd.rearrange("q (a t) -> q a t", a=8)),
        waits=[("tr_v", 8 * NTILE)], incs=[("p1d", 16)])

    p.emit("gpsimd", lambda e: e.collective_compute(
        "AllGather", ALU.bypass, replica_groups=G8, ins=[lg_in.opt()], outs=[lg_ag.opt()]),
        waits=[("p1d", 64), ("wload", N_W1), ("inv0", 16)], incs=[("ag_lg", 1)])
    p.emit("gpsimd", lambda e: e.collective_compute(
        "AllGather", ALU.bypass, replica_groups=G8, ins=[xtok_in.opt()], outs=[xtok_ag.opt()]),
        waits=[("xn_dma", 16 * NTILE)], incs=[("ag_xt", 1)])
    p.emit("gpsimd", lambda e: e.collective_compute(
        "AllGather", ALU.bypass, replica_groups=G4, ins=[xd_in.opt()], outs=[xd_ag.opt()]),
        waits=[("p1d", 80)], incs=[("ag_xd", 1)])
    p.emit("gpsimd", lambda e: e.collective_compute(
        "AllGather", ALU.bypass, replica_groups=G4, ins=[ckv_in.opt()], outs=[ckv_ag.opt()]),
        incs=[("ag_ck", 1)])

    if maxphase < 2:
        es2_ = [esA, esW]
        for st_ in es2_:
            st_.close()
        _finalize(nc, es, p, block_gate=True)
        return nc
    for rp in range(8):
        p.emit("sync", lambda e, rp=rp: e.dma_start(
            out=pl_l[16 * rp:16 * (rp + 1), :].rearrange("j (a i) -> j a i", a=NE),
            in_=lg_ag[NE * rp:NE * (rp + 1), :].rearrange("a (j i) -> j a i", i=128)),
            waits=[("ag_lg", 1)], incs=[("pl_ld", 16)])

    if maxphase < 2.1:
        esA.close(); esW.close()
        _finalize(nc, es, p, block_gate=True)
        return nc

    def pe_(i):
        return pl_l[:, 128 * i:128 * (i + 1)]

    def rv(fn, waits=()):
        p.emit("vector", lambda e, fn=fn: (fn(GV(e)), None)[1], waits=waits,
               incs=[("rt_v", 1)])

    rv(lambda e: e.tensor_copy(pl_m[:], pe_(0)),
       waits=[("pl_ld", 128), ("msets2", N_MS2)])
    for i in range(1, 8):
        rv(lambda e, i=i: e.tensor_tensor(pl_m[:], pl_m[:], pe_(i), ALU.max))
    for i in range(8):
        def m1f(e, i=i):
            e.tensor_tensor(pl_m1[:, 128 * i:128 * (i + 1)], pe_(i), pl_m[:], ALU.is_ge)
            e.tensor_scalar(pl_t[:], pl_f1[:], -1.0, 1.0, ALU.mult, ALU.add)
            e.tensor_tensor(pl_m1[:, 128 * i:128 * (i + 1)],
                            pl_m1[:, 128 * i:128 * (i + 1)], pl_t[:], ALU.mult)
            return e.tensor_tensor(pl_f1[:], pl_f1[:],
                                   pl_m1[:, 128 * i:128 * (i + 1)], ALU.add)
        rv(m1f)
    for i in range(8):
        def mska(e, i=i):
            e.tensor_scalar_mul(pl_t[:], pl_m1[:, 128 * i:128 * (i + 1)], 1.0e30)
            return e.tensor_tensor(pl_m2[:, 128 * i:128 * (i + 1)], pe_(i), pl_t[:],
                                   ALU.subtract)
        rv(mska)
    rv(lambda e: e.tensor_copy(pl_u[:], pl_m2[:, 0:128]))
    for i in range(1, 8):
        rv(lambda e, i=i: e.tensor_tensor(pl_u[:], pl_u[:],
                                          pl_m2[:, 128 * i:128 * (i + 1)], ALU.max))
    for i in range(8):
        def m2f(e, i=i):
            e.tensor_tensor(pl_m2[:, 128 * i:128 * (i + 1)],
                            pl_m2[:, 128 * i:128 * (i + 1)], pl_u[:], ALU.is_ge)
            e.tensor_scalar(pl_t[:], pl_f2[:], -1.0, 1.0, ALU.mult, ALU.add)
            e.tensor_tensor(pl_m2[:, 128 * i:128 * (i + 1)],
                            pl_m2[:, 128 * i:128 * (i + 1)], pl_t[:], ALU.mult)
            return e.tensor_tensor(pl_f2[:], pl_f2[:],
                                   pl_m2[:, 128 * i:128 * (i + 1)], ALU.add)
        rv(m2f)
    rv(lambda e: e.tensor_tensor(pl_t[:], pl_u[:], pl_m[:], ALU.subtract))
    p.emit("scalar", lambda e: (GS(e).activation(pl_x[:], pl_t[:], AF.Exp), None)[1],
           waits=[("rt_v", p.n("rt_v"))], incs=[("rt_s", 1)])

    def wts(e):
        e.tensor_scalar_add(pl_t[:], pl_x[:], 1.0)
        e.reciprocal(pl_t[:], pl_t[:])
        e.tensor_tensor(pl_u[:], pl_x[:], pl_t[:], ALU.mult)
        return e.tensor_scalar(pl_x[:], pl_u[:], -1.0, 1.0, ALU.mult, ALU.add)
    rv(wts, waits=[("rt_s", 1)])
    for i in range(8):
        def w8f(e, i=i):
            e.tensor_tensor(pl_w8[:, 128 * i:128 * (i + 1)],
                            pl_m1[:, 128 * i:128 * (i + 1)], pl_x[:], ALU.mult)
            e.tensor_tensor(pl_t[:], pl_m2[:, 128 * i:128 * (i + 1)], pl_u[:], ALU.mult)
            return e.tensor_tensor(pl_w8[:, 128 * i:128 * (i + 1)],
                                   pl_w8[:, 128 * i:128 * (i + 1)], pl_t[:], ALU.add)
        rv(w8f)

    if maxphase < 2.3:
        esA.close(); esW.close()
        _finalize(nc, es, p, block_gate=True)
        return nc
    for ee in range(NE):
        def rnd_a(e, ee=ee):
            e.tensor_tensor(pl_elig[:], pl_m1[:, 128 * ee:128 * (ee + 1)],
                            pl_m2[:, 128 * ee:128 * (ee + 1)], ALU.add)
            e.tensor_scalar(pl_t[:], pl_fb[:], -1.0, 1.0, ALU.mult, ALU.add)
            e.tensor_tensor(pl_elig[:], pl_elig[:], pl_t[:], ALU.mult)
            e.tensor_tensor_scan(pl_scan[:], pl_elig[:], pl_elig[:], 0.0,
                                 ALU.add, ALU.bypass)
            return e.tensor_copy(pl_bs[:], pl_scan[:, 127:128])
        rv(rnd_a)
        p.emit("tensor", lambda e: e.matmul(pf[5][0:128, 0:1], uexcl[:], pl_bs[:],
                                            start=True, stop=True),
               waits=[("rt_v", p.n("rt_v"))] + ([("rt_v2", p.n("rt_v2"))] if ee else []),
               incs=[("rt_pe", 1)])

        def rnd_b(e, ee=ee):
            e = GV(e)
            e.tensor_copy(pl_excl[:], pf[5][0:128, 0:1])
            e.tensor_scalar(pl_pos[:], pl_scan[:], pl_excl[:], None, ALU.add)
            e.tensor_scalar(pl_t[:], pl_pos[:], float(CAP), None, ALU.is_le)
            e.tensor_tensor(pl_kept[:], pl_t[:], pl_elig[:], ALU.mult)
            e.tensor_tensor(pl_t[:], pl_elig[:], pl_kept[:], ALU.subtract)
            e.tensor_tensor(pl_fb[:], pl_fb[:], pl_t[:], ALU.add)
            e.tensor_scalar(pl_u[:], pl_pos[:], -1.0 + ee * SLOTS, None, ALU.add)
            e.tensor_scalar(pl_t[:], pl_cnt[:], 0.0, None, ALU.is_equal)
            e.tensor_tensor(pl_sel[:], pl_t[:], pl_kept[:], ALU.mult)
            e.tensor_tensor(pl_x[:], pl_u[:], pl_offa[:], ALU.subtract)
            e.tensor_tensor(pl_x[:], pl_x[:], pl_sel[:], ALU.mult)
            e.tensor_tensor(pl_offa[:], pl_offa[:], pl_x[:], ALU.add)
            e.tensor_tensor(pl_x[:], pl_w8[:, 128 * ee:128 * (ee + 1)], pl_wa[:],
                            ALU.subtract)
            e.tensor_tensor(pl_x[:], pl_x[:], pl_sel[:], ALU.mult)
            e.tensor_tensor(pl_wa[:], pl_wa[:], pl_x[:], ALU.add)
            e.tensor_scalar(pl_t[:], pl_cnt[:], 1.0, None, ALU.is_equal)
            e.tensor_tensor(pl_sel[:], pl_t[:], pl_kept[:], ALU.mult)
            e.tensor_tensor(pl_x[:], pl_u[:], pl_offb[:], ALU.subtract)
            e.tensor_tensor(pl_x[:], pl_x[:], pl_sel[:], ALU.mult)
            e.tensor_tensor(pl_offb[:], pl_offb[:], pl_x[:], ALU.add)
            e.tensor_tensor(pl_x[:], pl_w8[:, 128 * ee:128 * (ee + 1)], pl_wb[:],
                            ALU.subtract)
            e.tensor_tensor(pl_x[:], pl_x[:], pl_sel[:], ALU.mult)
            e.tensor_tensor(pl_wb[:], pl_wb[:], pl_x[:], ALU.add)
            e.tensor_tensor(pl_cnt[:], pl_cnt[:], pl_kept[:], ALU.add)
            e.tensor_scalar(pl_t[:], pl_pos[:], ohsel[:, ee:ee + 1], None, ALU.mult)
            e.tensor_tensor(pl_posmy[:], pl_posmy[:], pl_t[:], ALU.add)
            e.tensor_scalar(pl_t[:], pl_kept[:], ohsel[:, ee:ee + 1], None, ALU.mult)
            e.tensor_tensor(pl_keptmy[:], pl_keptmy[:], pl_t[:], ALU.add)
        p.emit("vector", rnd_b, waits=[("rt_pe", ee + 1)], incs=[("rt_v2", 1)])

    def post(e):
        e = GV(e)
        e.tensor_scalar(pl_t[:], pl_fb[:], -1.0, 1.0, ALU.mult, ALU.add)
        e.tensor_tensor(pl_wa[:], pl_wa[:], pl_t[:], ALU.mult)
        e.tensor_tensor(pl_wb[:], pl_wb[:], pl_t[:], ALU.mult)
        e.tensor_scalar(pl_u[:], pl_posmy[:], -1.0, 0.0, ALU.add, ALU.max)
        e.tensor_copy(pl_iu[:], pl_u[:])
        e.tensor_scalar(pl_iv[:], pl_iu[:], 127, None, ALU.bitwise_and)
        e.tensor_copy(pl_x[:], pl_iv[:])
        e.tensor_scalar(pl_iv[:], pl_iu[:], 7, None, ALU.logical_shift_right)
        e.tensor_copy(pl_t[:], pl_iv[:])
        e.tensor_scalar(pl_osc[:], pl_x[:], float(NSC), None, ALU.mult)
        e.tensor_tensor(pl_osc[:], pl_osc[:], pl_t[:], ALU.add)
        e.tensor_scalar_add(pl_osc[:], pl_osc[:], -OORF)
        e.tensor_tensor(pl_osc[:], pl_osc[:], pl_keptmy[:], ALU.mult)
        e.tensor_scalar_add(pl_osc[:], pl_osc[:], OORF)
    p.emit("vector", post, waits=[("rt_v2", 8)], incs=[("rt_v3", 1)])

    if debug:
        for nm, src in [("dbg_pos", pl_posmy), ("dbg_kept", pl_keptmy),
                        ("dbg_fb", pl_fb), ("dbg_w8", pl_w8)]:
            p.emit("sync", lambda e, nm=nm, s=src: e.dma_start(out=dbg[nm][:], in_=s[:]),
                   waits=[("rt_v3", 1)], incs=[("dbg_s", 16)])

    if maxphase < 2.6:
        esA.close(); esW.close()
        _finalize(nc, es, p, block_gate=True)
        return nc
    plans = [(pl_osc, osct, None), (pl_offa, trfu, offa_d[:]), (pl_offb, trfu, offb_d[:]),
             (pl_wa, trf, wab_d[:, 0:128]), (pl_wb, trf, wab_d[:, 128:256]),
             (pl_fb, trf, wab_d[:, 256:384])]
    for k, (spl, ssb, sdr) in enumerate(plans):
        w = [("rt_v3", 1)]
        if k:
            w.append(("tp_v", k))
        p.emit("tensor", lambda e, s=spl: e.transpose(pf[5][0:128, 0:128], s[:], identf[:]),
               waits=w, incs=[("tp_pe", 1)])
        w2_ = [("tp_pe", k + 1)]
        if k >= 2:
            w2_.append(("tp_dma", p.n("tp_dma")))
        p.emit("vector", lambda e, d=ssb: (GV(e).tensor_copy(d[:], pf[5][0:128, 0:128]), None)[1],
               waits=w2_, incs=[("tp_v", 1)])
        if sdr is not None:
            p.emit("sync", lambda e, sb=ssb, dr=sdr: e.dma_start(out=dr, in_=sb[:]),
                   waits=[("tp_v", p.n("tp_v"))], incs=[("tp_dma", 16)])
    N_TPD = p.n("tp_dma")

    if maxphase < 2.8:
        esA.close(); esW.close()
        _finalize(nc, es, p, block_gate=True)
        return nc

    def mkrank(e):
        ctx["rk8"] = e.cc_rank(G8)
        ctx["rk4"] = e.cc_rank(G4)
        return e.dma_start(out=offa_o[:], in_=offa_d[:, ds(ctx["rk8"] * 16, 16)])
    p.emit("sync", mkrank, waits=[("tp_dma", N_TPD)], incs=[("own_ld", 16)])
    p.emit("sync", lambda e: e.dma_start(out=offb_o[:],
                                         in_=offb_d[:, ds(ctx["rk8"] * 16, 16)]),
           incs=[("own_ld", 16)])
    for k in range(3):
        p.emit("sync", lambda e, k=k: e.dma_start(
            out=wab_o[:, 16 * k:16 * (k + 1)],
            in_=wab_d[:, ds(ctx["rk8"] * 16 + 128 * k, 16)]),
            incs=[("own_ld", 16)])
    N_OWNLD = p.n("own_ld")

    if debug:
        p.emit("sync", lambda e: e.dma_start(out=dbg["dbg_offa"][:], in_=offa_d[:]),
               waits=[("tp_dma", 32)], incs=[("dbg_s", 16)])

    if maxphase < 3:
        esA.close(); esW.close()
        _finalize(nc, es, p, block_gate=True)
        return nc
    p.emit("gpsimd", lambda e: e.iota(payj[:], [[128, 128]], base=0, channel_multiplier=1),
           incs=[("disp", 1)])

    sc_h = p.bump("scat", 16 * 128)

    def scat_all(e):
        bc = e.to_reg(SLOTS - 1)
        for j in range(128):
            e.indirect_dma_start(
                out=inv_dd[:], out_offset=IndirectOffsetOnAxis(ap=osct[:, j:j + 1], axis=0),
                in_=payj[:, j:j + 1], in_offset=None,
                bounds_check=bc, oob_is_err=False).then_inc(sc_h, 16)
        return e.nop()
    p.emit("gpsimd", scat_all, waits=[("tp_v", 1), ("disp", 1), ("inv0", 16), ("ag_lg", 1), ("ag_xt", 1), ("ag_xd", 1), ("ag_ck", 1)])
    p.emit("gpsimd", lambda e: e.dma_start(
        out=invs[:], in_=inv_dd.rearrange("(b a) c -> b (a c)", b=128)),
        waits=[("scat", 16 * 128)], incs=[("invl", 16)])
    if debug:
        p.emit("sync", lambda e: e.dma_start(out=dbg["dbg_inv"][:], in_=invs[:]),
               waits=[("invl", 16)], incs=[("dbg_s", 16)])

    if maxphase < 4:
        esA.close(); esW.close()
        _finalize(nc, es, p, block_gate=True)
        return nc
    FENB = [("rt_v3", 1), ("tp_v", 6), ("invl", 16), ("kr_v", 6 * NTILE),
            ("proj_v", 2 * NTILE), ("ln_v", 3 * NTILE), ("bT_v", N_BT),
            ("wrt_v", 8), ("dbg_s", p.n("dbg_s"))]
    esA.close()
    esB = ExitStack()
    w2sb = S(esB, "w2sb", [128, 32 * D], BF16)
    hsb = S(esB, "hsb", [128, 32 * 256], BF16)
    ysb = S(esB, "ysb", [128, D], F32)
    xg = [S(esB, f"xg{i}", [128, D], BF16) for i in range(2)]
    xdt = S(esB, "xdt", [128, 8 * 256], BF16)

    wload(w2sb.rearrange("q (a c) -> q a c", a=32),
          w2_d.rearrange("(a q) c -> q a c", q=128),
          waits=FENB + [("ag_lg", 1), ("ag_xt", 1), ("ag_xd", 1), ("ag_ck", 1)])
    N_W2 = p.n("wload")

    p.emit("gpsimd", lambda e: (ctx.__setitem__("bcg", e.to_reg(TT - 1)), e.nop())[1],
           waits=FENB, incs=[("bcr", 1)])
    for gi in range(2):
        p.emit("vector", lambda e, gi=gi: (GV(e).memset(xg[gi][:], 0.0), None)[1],
               waits=(FENB if gi == 0 else []), incs=[("xgz", 1)])

    for st in range(p4st):
        for kk in range(2):
            k = 2 * st + kk
            w = [("ag_xt", 1), ("invl", 16), ("bcr", 1), ("xgz", 2)]
            if k >= 2:
                w.append(("xdv", 8 * (k - 1)))
            p.emit("gpsimd", lambda e, k=k: e.indirect_dma_start(
                out=xg[k % 2][:], out_offset=None, in_=xtok_ag[:],
                in_offset=IndirectOffsetOnAxis(ap=invs[:, k:k + 1], axis=0),
                bounds_check=ctx["bcg"], oob_is_err=False),
                waits=w, incs=[("gx", 16)])
            for ci in range(8):
                tcnt = k * 8 + ci
                pbx = pb0 if tcnt % 2 == 0 else pb1
                w = [("gx", 16 * (k + 1))]
                if tcnt >= 2:
                    w.append(("xdv", tcnt - 1))
                p.emit("tensor", lambda e, k=k, ci=ci, pbx=pbx: e.transpose(
                    pbx[:, 0:128], xg[k % 2][:, 128 * ci:128 * (ci + 1)], identb[:]),
                    waits=w, incs=[("xdtr", 1)])
                wv = [("xdtr", tcnt + 1)]
                if st >= 1 and kk == 0 and ci == 0:
                    wv.append(("hml", 32 * st))
                p.emit("vector", lambda e, kk=kk, ci=ci, pbx=pbx: (GV(e).tensor_copy(
                    xdt[:, 256 * ci + 128 * kk:256 * ci + 128 * (kk + 1)], pbx[:, 0:128]), None)[1],
                    waits=wv, incs=[("xdv", 1)])
        for ic in range(32):
            pfh = pf[ic % 2]
            hcnt = st * 32 + ic
            for ci in range(8):
                w = []
                if ci == 0:
                    w = [("xdv", 16 * (st + 1))]
                    if st == 0 and ic == 0:
                        w.append(("wload", N_W1))
                    if hcnt >= 2:
                        w.append(("gel", hcnt - 1))
                p.emit("tensor", lambda e, ci=ci, ic=ic, pfh=pfh: e.matmul(
                    pfh[0:128, 0:256],
                    w1sb[:, 4096 * ci + 128 * ic:4096 * ci + 128 * (ic + 1)],
                    xdt[:, 256 * ci:256 * (ci + 1)], start=(ci == 0), stop=(ci == 7)),
                    waits=w, incs=([("hml", 1)] if ci == 7 else []))
            wsc = [("hml", hcnt + 1)]
            if st == 0 and ic == 0:
                wsc.append(("bT_v", N_BT))
            if st >= 1 and ic == 0:
                wsc.append(("ymm", 4 * st))
            p.emit("scalar", lambda e, ic=ic, pfh=pfh: (GS(e).activation(
                hsb[:, 256 * ic:256 * (ic + 1)], pfh[0:128, 0:256], AF.Gelu,
                bias=b1t[:, ic:ic + 1]), None)[1], waits=wsc, incs=[("gel", 1)])
        for ss in range(2):
            for dh in range(2):
                pfy = pf[2 + (2 * ss + dh) % 2]
                ycnt = st * 4 + ss * 2 + dh
                for ic in range(32):
                    w = []
                    if ic == 0:
                        w = [("gel", 32 * (st + 1))]
                        if st == 0 and ss == 0 and dh == 0:
                            w.append(("wload", N_W2))
                        if ycnt >= 2:
                            w.append(("yev", ycnt - 1))
                    p.emit("tensor", lambda e, ss=ss, dh=dh, ic=ic, pfy=pfy: e.matmul(
                        pfy[:], hsb[:, 256 * ic + 128 * ss:256 * ic + 128 * (ss + 1)],
                        w2sb[:, 1024 * ic + 512 * dh:1024 * ic + 512 * (dh + 1)],
                        start=(ic == 0), stop=False), waits=w)
                w = [("msets", 4), ("wload", 112)] if ycnt == 0 else []
                p.emit("tensor", lambda e, dh=dh, pfy=pfy: e.matmul(
                    pfy[:], onesb[0:1, 0:128], b2r[0:1, 512 * dh:512 * (dh + 1)],
                    start=False, stop=True), waits=w, incs=[("ymm", 1)])
                p.emit("vector", lambda e, dh=dh, pfy=pfy: (GV(e).tensor_copy(
                    ysb[:, 512 * dh:512 * (dh + 1)], pfy[:]), None)[1],
                    waits=[("ymm", ycnt + 1), ("ydma", 16 * (st * 2 + ss))],
                    incs=[("yev", 1)])
            p.emit("sync", lambda e, st=st, ss=ss: e.dma_start(
                out=y_in[256 * st + 128 * ss:256 * st + 128 * (ss + 1), :], in_=ysb[:]),
                waits=[("yev", st * 4 + 2 * (ss + 1))], incs=[("ydma", 16)])
    N_YDMA = p.n("ydma")

    p.emit("gpsimd", lambda e: e.collective_compute(
        "AllGather", ALU.bypass, replica_groups=G8, ins=[y_in.opt()], outs=[y_ag.opt()]),
        waits=[("ydma", N_YDMA)], incs=[("ag_y", 1)])
    if debug:
        p.emit("sync", lambda e: e.dma_start(out=dbg["dbg_y"][:], in_=y_in[:]),
               waits=[("ydma", N_YDMA)], incs=[("dbg_s", 16)])

    if maxphase < 5:
        esB.close(); esW.close()
        _finalize(nc, es, p, block_gate=True)
        return nc
    FENC = [("ymm", p.n("ymm")), ("gel", p.n("gel")), ("xdv", p.n("xdv")),
            ("yev", p.n("yev")), ("ydma", N_YDMA), ("gx", p.n("gx")),
            ("xdtr", p.n("xdtr")), ("hml", p.n("hml")), ("dbg_s", p.n("dbg_s"))]
    esB.close()
    esW.close()
    esP.close()
    esC = ExitStack()
    ckvb = S(esC, "ckvb", [128, T], BF16)
    kaug = S(esC, "kaug", [128, T], BF16)
    vaug = S(esC, "vaug", [128, 64 * 260], BF16)
    qpair = S(esC, "qpair", [128, 16 * 2 * 512], BF16)
    rope32 = S(esC, "rope32", [32, 16 * 512], BF16)
    xql = [S(esC, f"xql{i}", [128, 8 * 512], BF16) for i in range(2)]
    exps = [S(esC, f"exps{i}", [128, 512], BF16) for i in range(2)]
    expsm = S(esC, "expsm", [128, 512], BF16)
    recd = S(esC, "recd", [1, 512], F32)
    bcsb = S(esC, "bcsb", [64, 512], F32)
    acps = S(esC, "acps", [64, 512], BF16)
    dmask2 = S(esC, "dmask2", [128, 4 * 512], BF16)
    cq = S(esC, "cq", [16, 512], F32)
    sq = S(esC, "sq", [16, 512], F32)
    rt5a = S(esC, "rt5a", [16, 512], F32)
    rt5b = S(esC, "rt5b", [16, 512], F32)
    rt5c = S(esC, "rt5c", [16, 512], BF16)
    pk = esC.enter_context(nc.psum_tensor("pk", [64, 512], F32))
    pbc = esC.enter_context(nc.psum_tensor("pbc", [64, 512], F32))

    p.emit("sync", lambda e: e.dma_start(out=dmask2[:], in_=dmask_d[:]),
           waits=FENC, incs=[("c5", 16)])
    p.emit("gpsimd", lambda e: e.memset(vaug[:], 1.0), waits=FENC, incs=[("c5", 1)])
    N_C5 = p.n("c5")
    if maxphase < 5.12:
        esC.close()
        _finalize(nc, es, p, block_gate=True)
        return nc
    for kc16 in range(16):
        rq, cs = kc16 // 4, 512 * (kc16 % 4)
        w = [("ag_ck", 1)] + (FENC if kc16 == 0 else [])
        p.emit("sync", lambda e, kc16=kc16, rq=rq, cs=cs: e.dma_start(
            out=ckvb[:, 512 * kc16:512 * (kc16 + 1)],
            in_=ckv_ag[160 * rq:160 * rq + 128, cs:cs + 512]), waits=w, incs=[("ckl", 16)])
        p.emit("sync", lambda e, kc16=kc16, rq=rq, cs=cs: e.dma_start(
            out=kaug[64:80, 512 * kc16:512 * (kc16 + 1)],
            in_=ckv_ag[160 * rq + 128:160 * rq + 144, cs:cs + 512]), incs=[("ckl", 16)])
        p.emit("sync", lambda e, kc16=kc16, rq=rq, cs=cs: e.dma_start(
            out=kaug[80:96, 512 * kc16:512 * (kc16 + 1)],
            in_=ckv_ag[160 * rq + 144:160 * rq + 160, cs:cs + 512]), incs=[("ckl", 16)])
    N_CKL = p.n("ckl")
    if maxphase < 5.15:
        esC.close()
        _finalize(nc, es, p, block_gate=True)
        return nc
    for kc in range(64):
        w = [("ckl", N_CKL), ("wload", 64), ("c5", N_C5)] if kc == 0 else \
            [("vdv", kc)]
        p.emit("tensor", lambda e, kc=kc: e.matmul(
            pf[4][0:128, 0:256], ckvb[:, 128 * kc:128 * (kc + 1)], wvup4[:],
            start=True, stop=True), waits=w, incs=[("vdp", 1)])
        p.emit("vector", lambda e, kc=kc: (GV(e).tensor_copy(
            vaug[:, 260 * kc:260 * kc + 260].rearrange("q (h z) -> q h z", z=65)[:, :, 0:64],
            pf[4][0:128, 0:256].rearrange("q (h z) -> q h z", z=64)), None)[1],
            waits=[("vdp", kc + 1)], incs=[("vdv", 1)])
    N_VDV = p.n("vdv")

    if maxphase < 5.2:
        esC.close()
        _finalize(nc, es, p, block_gate=True)
        return nc

    for p01 in range(2):
        qpe_hist = []
        for qc in range(16):
            rq, cs = qc // 4, 512 * (qc % 4)
            w = [("ag_xd", 1)]
            if p01 == 0 and qc == 0:
                w += FENC
            nqpe_hist = p.n("qpe")
            if qc >= 2:
                w.append(("qpe", qpe_hist[qc - 2]))
            elif p01 == 1 and qc < 2:
                w.append(("qpe", p.n("qpe")))
            p.emit("sync", lambda e, qc=qc, rq=rq, cs=cs: e.dma_start(
                out=xql[qc % 2][:].rearrange("q (a t) -> q a t", a=8),
                in_=xd_ag[1024 * rq:1024 * (rq + 1), cs:cs + 512]
                    .rearrange("(a q) t -> q a t", q=128)),
                waits=w, incs=[("qld", 16)])
            NQLD = p.n("qld")
            for ci in range(8):
                w = []
                if ci == 0:
                    w = [("qld", NQLD), ("qv", p.n("qv"))]
                    if p01 == 0 and qc == 0:
                        w.append(("wload", 16))
                p.emit("tensor", lambda e, qc=qc, ci=ci, p01=p01: e.matmul(
                    pf[5][:], wq4[:, 256 * ci + 128 * p01:256 * ci + 128 * (p01 + 1)],
                    xql[qc % 2][:, 512 * ci:512 * (ci + 1)],
                    start=(ci == 0), stop=(ci == 7)),
                    waits=w,
                    incs=([("qpe", 1)] if (ci == 7 and (p01 == 1 or maxphase < 5.25)) else []))
            if p01 == 0 and maxphase >= 5.25:
                for half in range(2):
                    pfr = pf[4] if half == 0 else pf[3]
                    for ci in range(8):
                        w = []
                        if ci == 0 and half == 0:
                            w = [("rp_v", p.n("rp_v")), ("vdv", N_VDV)]
                        p.emit("tensor", lambda e, ci=ci, half=half, qc=qc, pfr=pfr:
                               e.matmul(pfr[0:16, 0:512],
                                        wqr[:, 32 * ci + 16 * half:32 * ci + 16 * (half + 1)],
                                        xql[qc % 2][:, 512 * ci:512 * (ci + 1)],
                                        start=(ci == 0), stop=(ci == 7)),
                               waits=w,
                               incs=([("qpe", 1)] if (ci == 7 and half == 1) else []))

                p.emit("sync", lambda e, qc=qc: e.dma_start(
                    out=cq[:], in_=cos_full[:, 512 * qc:512 * (qc + 1)]),
                    waits=[("rp_v", p.n("rp_v"))], incs=[("cql", 16)])
                p.emit("sync", lambda e, qc=qc: e.dma_start(
                    out=sq[:], in_=sin_full[:, 512 * qc:512 * (qc + 1)]),
                    incs=[("cql", 16)])

                def rop5(e, qc=qc):
                    e = GV(e)
                    c0, c1 = 512 * qc, 512 * (qc + 1)
                    e.tensor_tensor(rt5a[:], pf[4][0:16, 0:512], cq[:], ALU.mult)
                    e.tensor_tensor(rt5b[:], pf[3][0:16, 0:512], sq[:], ALU.mult)
                    e.tensor_tensor(rope32[0:16, c0:c1], rt5a[:], rt5b[:], ALU.subtract)
                    e.tensor_tensor(rt5a[:], pf[3][0:16, 0:512], cq[:], ALU.mult)
                    e.tensor_tensor(rt5b[:], pf[4][0:16, 0:512], sq[:], ALU.mult)
                    e.tensor_tensor(rt5c[:], rt5a[:], rt5b[:], ALU.add)
                p.emit("vector", rop5, waits=[("qpe", p.n("qpe")), ("rp_d", p.n("rp_d")),
                                              ("cql", p.n("cql"))],
                       incs=[("rp_v", 1)])
                p.emit("sync", lambda e, qc=qc: e.dma_start(
                    out=rope32[16:32, 512 * qc:512 * (qc + 1)], in_=rt5c[:]),
                    waits=[("rp_v", p.n("rp_v"))], incs=[("rp_d", 16)])
            qpe_hist.append(p.n("qpe"))
            for s01 in range(2):
                w = [("qpe", qpe_hist[-1])]
                if p01 == 1 and qc == 0 and s01 == 0:
                    w.append(("sc_pe", p.n("sc_pe")))
                p.emit("vector", lambda e, qc=qc, s01=s01: (GV(e).tensor_copy(
                    qpair[0:64, 512 * (2 * qc + s01):512 * (2 * qc + s01) + 512],
                    pf[5][64 * s01:64 * (s01 + 1), :]), None)[1], waits=w, incs=[("qv", 1)])
                if maxphase >= 5.27:
                    wr_ = [("qv", p.n("qv"))]
                    if p01 == 0:
                        wr_.append(("rp_d", p.n("rp_d")))
                    else:
                        wr_.append(("rp_d", 16 * 16))
                    p.emit("vector", lambda e, qc=qc, s01=s01: (GV(e).tensor_copy(
                        qpair[64:96, 512 * (2 * qc + s01):512 * (2 * qc + s01) + 512],
                        rope32[:, 512 * qc:512 * (qc + 1)]), None)[1], waits=wr_, incs=[("qv", 1)])
        NQV = p.n("qv")
        if maxphase < 5.3:
            break
        for s01 in range(2):
            hl = 2 * p01 + s01
            for kc5 in range(16):
                w = [("kdv", p.n("kdv"))]
                if kc5 == 0:
                    w += [("ckl", N_CKL), ("wload", 48), ("sc_pe", p.n("sc_pe")),
                          ("vdv", N_VDV)]
                p.emit("tensor", lambda e, hl=hl, kc5=kc5: e.matmul(
                    pk[:], wkup4[:, 64 * hl:64 * (hl + 1)],
                    ckvb[:, 512 * kc5:512 * (kc5 + 1)], start=True, stop=True),
                    waits=w, incs=[("kdp", 1)])
                p.emit("vector", lambda e, kc5=kc5: (GV(e).tensor_copy(
                    kaug[0:64, 512 * kc5:512 * (kc5 + 1)], pk[:]), None)[1],
                    waits=[("kdp", p.n("kdp"))], incs=[("kdv", 1)])
            NKDV = p.n("kdv")
            if maxphase < 5.4:
                continue
            for qc in range(16):
                nkt = 4 * qc + 4
                qcol = 512 * (2 * qc + s01)
                for kt in range(nkt):
                    sbank = pf[kt % 2]
                    exn = p.n("ex_s")
                    w = [("ex_s", exn - 1 if kt >= 2 else exn)]
                    if kt == 0:
                        w += [("kdv", NKDV), ("qv", NQV), ("ep_v", p.n("ep_v"))]
                    p.emit("tensor", lambda e, kt=kt, qcol=qcol, sbank=sbank: e.matmul(
                        sbank[:], kaug[0:96, 128 * kt:128 * (kt + 1)],
                        qpair[0:96, qcol:qcol + 512], start=True, stop=True),
                        waits=w, incs=[("sc_pe", 1)])
                    avn = p.n("av_k")
                    wexp = [("sc_pe", p.n("sc_pe")), ("msets", 4)]
                    if maxphase >= 5.45:
                        wexp.append(("av_k", avn - 1 if kt >= 2 else avn))
                    p.emit("scalar", lambda e, kt=kt, sbank=sbank: (GS(e).activation(
                        exps[kt % 2][:], sbank[:], AF.Exp, bias=bias25[:], scale=ISCALE),
                        None)[1], waits=wexp, incs=[("ex_s", 1)])
                    if maxphase < 5.45:
                        continue
                    diag = kt >= 4 * qc
                    if diag:
                        p.emit("vector", lambda e, kt=kt, qc=qc: (GV(e).tensor_tensor(
                            expsm[:], exps[kt % 2][:],
                            dmask2[:, 512 * (kt - 4 * qc):512 * (kt - 4 * qc + 1)],
                            ALU.mult), None)[1],
                            waits=[("ex_s", p.n("ex_s")), ("av_k", p.n("av_k")),
                                   ("c5", N_C5)],
                            incs=[("mk_v", 1)])
                        rhs_t = expsm
                        wav = [("mk_v", p.n("mk_v"))]
                    else:
                        rhs_t = exps[kt % 2]
                        wav = [("ex_s", p.n("ex_s"))]
                    p.emit("tensor", lambda e, kt=kt, hl=hl, rhs_t=rhs_t, nkt=nkt: e.matmul(
                        pf[2][0:65, :], vaug[:, 260 * kt + 65 * hl:260 * kt + 65 * hl + 65],
                        rhs_t[:], start=(kt == 0), stop=(kt == nkt - 1)),
                        waits=wav, incs=[("av_k", 1)])
                if maxphase < 5.48:
                    continue
                p.emit("vector", lambda e: (GV(e).reciprocal(recd[:], pf[2][64:65, :]), None)[1],
                       waits=[("av_k", p.n("av_k"))], incs=[("ep_v", 1)])
                p.emit("tensor", lambda e: e.matmul(
                    pbc[:], onesf[:], recd[:], start=True, stop=True),
                    waits=[("ep_v", p.n("ep_v"))], incs=[("bc_pe", 1)])
                p.emit("vector", lambda e: (GV(e).tensor_copy(bcsb[:], pbc[:]), None)[1],
                       waits=[("bc_pe", p.n("bc_pe"))], incs=[("ep_v", 1)])
                p.emit("vector", lambda e: (GV(e).tensor_tensor(
                    acps[:], pf[2][0:64, :], bcsb[:], ALU.mult), None)[1],
                    waits=[("ep_v", p.n("ep_v")), ("ac_d", p.n("ac_d"))],
                    incs=[("ep_v", 1)])
                p.emit("sync", lambda e, hl=hl, qc=qc: e.dma_start(
                    out=ac_in[64 * hl:64 * (hl + 1), 512 * qc:512 * (qc + 1)],
                    in_=acps[:]),
                    waits=[("ep_v", p.n("ep_v"))], incs=[("ac_d", 16)])
            if maxphase < 5.5:
                break
        if maxphase < 5.8:
            break
    N_ACD = p.n("ac_d")
    if maxphase < 6 and maxphase != int(maxphase):
        esC.close()
        _finalize(nc, es, p, block_gate=True)
        return nc

    p.emit("gpsimd", lambda e: e.collective_compute(
        "AllGather", ALU.bypass, replica_groups=G4, ins=[ac_in.opt()], outs=[ac_ag.opt()]),
        waits=[("ac_d", N_ACD)], incs=[("ag_ac", 1)])
    if debug:
        p.emit("sync", lambda e: e.dma_start(out=dbg["dbg_ac"][:], in_=ac_in[:]),
               waits=[("ac_d", N_ACD)], incs=[("dbg_s", 16)])

    if maxphase < 6:
        esC.close()
        _finalize(nc, es, p, block_gate=True)
        return nc
    FEND = [("ep_v", p.n("ep_v")), ("ac_d", N_ACD), ("qv", p.n("qv")),
            ("sc_pe", p.n("sc_pe")), ("av_k", p.n("av_k")), ("ex_s", p.n("ex_s")),
            ("mk_v", p.n("mk_v")), ("kdv", p.n("kdv")), ("vdv", N_VDV),
            ("rp_v", p.n("rp_v")), ("qld", p.n("qld")), ("dbg_s", p.n("dbg_s"))]
    esC.close()
    esD = ExitStack()
    fw1sb = S(esD, "fw1sb", [128, 8 * DI], BF16)
    fw2sb = S(esD, "fw2sb", [128, 32 * D], BF16)
    hs2 = S(esD, "hs2", [128, 32 * 256], BF16)
    ys2 = S(esD, "ys2", [128, D], F32)

    wload_ch(fw1sb, fw1_d, D, DI, waits=FEND + [("ag_ac", 1), ("ag_y", 1)])
    N_FW1 = p.n("wload")
    wload(fw2sb.rearrange("q (a c) -> q a c", a=32),
          fw2_d.rearrange("(a q) c -> q a c", q=128))
    N_FW2 = p.n("wload")

    for tt8 in range(8):
        for ic in range(32):
            pfh = pf[ic % 2]
            hcnt = tt8 * 32 + ic
            for ci in range(8):
                w = []
                if ci == 0:
                    w = [("wload", N_FW1)] + (FEND if (tt8 == 0 and ic == 0) else [])
                    if hcnt >= 2:
                        w.append(("fgel", hcnt - 1))
                p.emit("tensor", lambda e, ci=ci, ic=ic, tt8=tt8, pfh=pfh: e.matmul(
                    pfh[0:128, 0:256],
                    fw1sb[:, 4096 * ci + 128 * ic:4096 * ci + 128 * (ic + 1)],
                    xnd[:, 2048 * ci + 256 * tt8:2048 * ci + 256 * (tt8 + 1)],
                    start=(ci == 0), stop=(ci == 7)),
                    waits=w, incs=([("fhm", 1)] if ci == 7 else []))
            w = [("fhm", hcnt + 1)]
            if tt8 >= 1 and ic == 0:
                w.append(("fym", 4 * tt8))
            p.emit("scalar", lambda e, ic=ic, pfh=pfh: (GS(e).activation(
                hs2[:, 256 * ic:256 * (ic + 1)], pfh[0:128, 0:256], AF.Gelu,
                bias=fb1t[:, ic:ic + 1]), None)[1], waits=w, incs=[("fgel", 1)])
        for ss in range(2):
            for dh in range(2):
                pfy = pf[2 + (2 * ss + dh) % 2]
                ycnt = tt8 * 4 + ss * 2 + dh
                for ic in range(32):
                    w = []
                    if ic == 0:
                        w = [("fgel", 32 * (tt8 + 1)), ("wload", N_FW2)]
                        if ycnt >= 2:
                            w.append(("fyv", ycnt - 1))
                    p.emit("tensor", lambda e, ss=ss, dh=dh, ic=ic, pfy=pfy: e.matmul(
                        pfy[:], hs2[:, 256 * ic + 128 * ss:256 * ic + 128 * (ss + 1)],
                        fw2sb[:, 1024 * ic + 512 * dh:1024 * ic + 512 * (dh + 1)],
                        start=(ic == 0), stop=False), waits=w)
                p.emit("tensor", lambda e, dh=dh, pfy=pfy: e.matmul(
                    pfy[:], onesb[0:1, 0:128], fb2r[0:1, 512 * dh:512 * (dh + 1)],
                    start=False, stop=True),
                    waits=([("wload", 128)] if ycnt == 0 else []), incs=[("fym", 1)])
                p.emit("vector", lambda e, dh=dh, pfy=pfy: (GV(e).tensor_copy(
                    ys2[:, 512 * dh:512 * (dh + 1)], pfy[:]), None)[1],
                    waits=[("fym", ycnt + 1), ("fyd", 16 * (tt8 * 2 + ss))],
                    incs=[("fyv", 1)])
            p.emit("sync", lambda e, tt8=tt8, ss=ss: e.dma_start(
                out=yfb_d[256 * tt8 + 128 * ss:256 * tt8 + 128 * (ss + 1), :], in_=ys2[:]),
                waits=[("fyv", tt8 * 4 + 2 * (ss + 1))], incs=[("fyd", 16)])
    N_FYD = p.n("fyd")
    if debug:
        p.emit("sync", lambda e: e.dma_start(out=dbg["dbg_yfb"][:], in_=yfb_d[:]),
               waits=[("fyd", N_FYD)], incs=[("dbg_s", 16)])

    if maxphase < 7:
        esD.close()
        _finalize(nc, es, p, block_gate=True)
        return nc
    FENE = [("fym", p.n("fym")), ("fgel", p.n("fgel")), ("fyv", p.n("fyv")),
            ("fyd", N_FYD), ("fhm", p.n("fhm")), ("dbg_s", p.n("dbg_s"))]
    esD.close()
    esE = ExitStack()
    wosb = S(esE, "wosb", [128, 8 * D], BF16)
    woact = S(esE, "woact", [128, 8 * OWN], BF16)
    ga = [S(esE, f"ga{i}", [128, D], F32) for i in range(2)]
    gb = [S(esE, f"gb{i}", [128, D], F32) for i in range(2)]
    yf2 = [S(esE, f"yf2{i}", [128, D], F32) for i in range(2)]
    xs2 = [S(esE, f"xs2{i}", [128, D], F32) for i in range(2)]
    cmb = S(esE, "cmb", [128, D], F32)
    cmt = S(esE, "cmt", [128, D], F32)
    outsb = [S(esE, f"outsb{i}", [128, D], F32) for i in range(2)]

    wload_ch(wosb, wo_d, D, D, waits=FENE + [("ag_ac", 1)])
    N_WO = p.n("wload")
    for blk in range(8):
        p.emit("sync", lambda e, blk=blk: e.dma_start(
            out=woact[:, 2048 * blk:2048 * (blk + 1)],
            in_=ac_ag[128 * blk:128 * (blk + 1), ds(ctx["rk4"] * 2048, 2048)]),
            waits=[("ag_ac", 1)] + (FENE if blk == 0 else []), incs=[("wol", 16)])
    N_WOL = p.n("wol")

    p.emit("gpsimd", lambda e: (ctx.__setitem__("bcy", e.to_reg(NE * SLOTS - 1)), e.nop())[1],
           waits=FENE, incs=[("bcr2", 1)])

    for k in range(16):
        bk = k % 2
        w = FENE + [("msets", 4)] if k == 0 else []
        if k >= 2:
            w = [("cmv", 3 * (k - 1))]
        p.emit("vector", lambda e, bk=bk: (GV(e).memset(ga[bk][:], 0.0),
                                           GV(e).memset(gb[bk][:], 0.0), None)[2],
               waits=w, incs=[("cmz", 1)])
        p.emit("gpsimd", lambda e, k=k, bk=bk: e.indirect_dma_start(
            out=ga[bk][:], out_offset=None, in_=y_ag[:],
            in_offset=IndirectOffsetOnAxis(ap=offa_o[:, k:k + 1], axis=0),
            bounds_check=ctx["bcy"], oob_is_err=False),
            waits=[("cmz", k + 1), ("ag_y", 1), ("ag_ac", 1), ("own_ld", N_OWNLD), ("bcr2", 1)],
            incs=[("cga", 16)])
        p.emit("gpsimd", lambda e, k=k, bk=bk: e.indirect_dma_start(
            out=gb[bk][:], out_offset=None, in_=y_ag[:],
            in_offset=IndirectOffsetOnAxis(ap=offb_o[:, k:k + 1], axis=0),
            bounds_check=ctx["bcy"], oob_is_err=False), incs=[("cga", 16)])
        wx = [("cmv", 3 * (k - 1))] if k >= 2 else []
        p.emit("sync", lambda e, k=k, bk=bk: e.dma_start(
            out=xs2[bk][:], in_=x_own[128 * k:128 * (k + 1), :]),
            waits=wx + ([("fyd", N_FYD)] if k == 0 else []), incs=[("xld2", 16)])
        p.emit("sync", lambda e, k=k, bk=bk: e.dma_start(
            out=yf2[bk][:], in_=yfb_d[128 * k:128 * (k + 1), :]), incs=[("xld2", 16)])
        for dh in range(2):
            for blk in range(8):
                w = []
                if blk == 0:
                    w = [("wol", N_WOL)]
                    if k == 0 and dh == 0:
                        w.append(("wload", N_WO))
                    if k >= 1:
                        w.append(("cmv", 3 * k))
                p.emit("tensor", lambda e, k=k, dh=dh, blk=blk: e.matmul(
                    pf[dh][:], woact[:, 2048 * blk + 128 * k:2048 * blk + 128 * (k + 1)],
                    wosb[:, 1024 * blk + 512 * dh:1024 * blk + 512 * (dh + 1)],
                    start=(blk == 0), stop=(blk == 7)),
                    waits=w, incs=([("wop", 1)] if blk == 7 else []))

        def comb(e, k=k, bk=bk):
            e = GV(e)
            e.tensor_scalar(cmb[:], ga[bk][:], wab_o[:, k:k + 1], None, ALU.mult)
            e.tensor_scalar(cmt[:], gb[bk][:], wab_o[:, 16 + k:16 + k + 1], None, ALU.mult)
            e.tensor_tensor(cmb[:], cmb[:], cmt[:], ALU.add)
            e.tensor_scalar(cmt[:], yf2[bk][:], wab_o[:, 32 + k:32 + k + 1], None, ALU.mult)
            e.tensor_tensor(cmb[:], cmb[:], cmt[:], ALU.add)
            e.tensor_tensor(cmb[:], cmb[:], xs2[bk][:], ALU.add)
        p.emit("vector", comb,
               waits=[("cga", 32 * (k + 1)), ("xld2", 32 * (k + 1)),
                      ("own_ld", N_OWNLD)], incs=[("cmv", 1)])
        for dh in range(2):
            wev = [("wop", 2 * (k + 1)), ("cmv", 3 * k + 1)]
            if k >= 2:
                wev.append(("stout", 16 * (k - 1)))
            p.emit("vector", lambda e, dh=dh, bk=bk: (GV(e).tensor_tensor(
                outsb[bk][:, 512 * dh:512 * (dh + 1)], pf[dh][:],
                cmb[:, 512 * dh:512 * (dh + 1)], ALU.add), None)[1],
                waits=wev, incs=[("cmv", 1)])
        p.emit("sync", lambda e, k=k, bk=bk: e.dma_start(
            out=out_d[128 * k:128 * (k + 1), :], in_=outsb[bk][:]),
            waits=[("cmv", 3 * (k + 1))], incs=[("stout", 16)])
    p.wait("sync", "stout", 16 * 16)

    esE.close()
    _finalize(nc, es, p)
    return nc


def _finalize(nc, es, p, block_gate=False):
    block = es.enter_context(nc.Block())

    @block.sync
    def _(sync):
        for f in p.q["sync"]:
            f(sync)

    @block.tensor
    def _(tensor):
        for f in p.q["tensor"]:
            f(tensor)

    @block.vector
    def _(vector):
        for f in p.q["vector"]:
            f(vector)

    @block.scalar
    def _(scalar):
        for f in p.q["scalar"]:
            f(scalar)

    @block.gpsimd
    def _(gpsimd):
        for f in p.q["gpsimd"]:
            f(gpsimd)

    es.close()


_BF = ml_dtypes.bfloat16


def _rope_tables():
    inv_freq = (1.0 / (10000.0 ** (np.arange(0, DR, 2, dtype=np.float32) / DR))
                ).astype(np.float32)
    t = np.arange(T, dtype=np.float32)
    freqs = t[:, None] * inv_freq[None, :]
    return np.cos(freqs).T.copy(), np.sin(freqs).T.copy()


def _consts():
    identb = np.eye(128, dtype=_BF)
    identf = np.eye(128, dtype=np.float32)
    uexcl = (np.arange(128)[:, None] < np.arange(128)[None, :]).astype(_BF)
    kk = np.arange(128)[:, None]
    qq = np.arange(512)[None, :]
    dmask = np.concatenate(
        [(kk <= qq - 128 * j).astype(_BF) for j in range(4)], axis=1)
    return identb, identf, uexcl, dmask


def kernel(L0_attn_x, L0_attn_W_kv_down, L0_attn_W_k_up, L0_attn_W_v_up, L0_attn_W_q,
           L0_attn_W_q_rope, L0_attn_W_k_rope, L0_attn_W_o, L0_ffn_Wr, L0_ffn_W1,
           L0_ffn_b1, L0_ffn_W2, L0_ffn_b2, L0_ffn_fw1, L0_ffn_fb1, L0_ffn_fw2,
           L0_ffn_fb2, _debug=0, _maxphase=7, _p4st=20):
    x = np.ascontiguousarray(np.asarray(L0_attn_x, dtype=np.float32))
    cosT, sinT = _rope_tables()
    identb, identf, uexcl, dmask = _consts()
    f32 = lambda a: np.ascontiguousarray(np.asarray(a, dtype=np.float32))
    Wq, Wkv = f32(L0_attn_W_q), f32(L0_attn_W_kv_down)
    Wkup, Wvup = f32(L0_attn_W_k_up), f32(L0_attn_W_v_up)
    Wqr, Wkr, Wo = f32(L0_attn_W_q_rope), f32(L0_attn_W_k_rope), f32(L0_attn_W_o)
    Wr, W1, b1 = f32(L0_ffn_Wr), f32(L0_ffn_W1), f32(L0_ffn_b1)
    W2, b2 = f32(L0_ffn_W2), f32(L0_ffn_b2)
    fw1, fb1, fw2, fb2 = f32(L0_ffn_fw1), f32(L0_ffn_fb1), f32(L0_ffn_fw2), f32(L0_ffn_fb2)

    nc = build(debug=_debug, maxphase=_maxphase, p4st=_p4st)
    in_maps = []
    for c in range(8):
        b, wq_ = c // 4, c % 4
        oh = np.zeros((128, NE), np.float32)
        oh[:, c] = 1.0
        in_maps.append({
            "x_own": x[b, 2048 * wq_:2048 * (wq_ + 1)].copy(),
            "cos_own": cosT[:, 2048 * wq_:2048 * (wq_ + 1)].copy(),
            "sin_own": sinT[:, 2048 * wq_:2048 * (wq_ + 1)].copy(),
            "cos_full": cosT, "sin_full": sinT,
            "wq4": Wq[:, 256 * wq_:256 * (wq_ + 1)].copy(),
            "wkv": Wkv, "wkup4": Wkup[:, 256 * wq_:256 * (wq_ + 1)].copy(),
            "wvup4": Wvup[:, 256 * wq_:256 * (wq_ + 1)].copy(),
            "wqr": Wqr, "wkr": Wkr, "wo": Wo, "wr": Wr,
            "w1e": W1[c], "b1e": b1[c], "w2e": W2[c], "b2e": b2[c],
            "fw1": fw1, "fb1": fb1, "fw2": fw2, "fb2": fb2,
            "identb": identb, "identf": identf, "uexcl": uexcl, "dmask": dmask,
            "ohsel": oh,
        })
    res = run_bass_kernel_spmd(nc, in_maps, core_ids=list(range(8)))
    out = np.empty((B, T, D), np.float32)
    for c in range(8):
        b, wq_ = c // 4, c % 4
        out[b, 2048 * wq_:2048 * (wq_ + 1)] = np.asarray(res.results[c]["out_own"])
    if _debug:
        kernel._dbg = res
    return out
